# version 10
# baseline (speedup 1.0000x reference)
import contextlib
import numpy as np
import concourse.bass as bass
import concourse.mybir as mybir
from concourse.aot_env import allow_ici_all_to_all
from concourse.bass_utils import run_bass_kernel_spmd

F32 = mybir.dt.float32
BF16 = mybir.dt.bfloat16
AF = mybir.ActivationFunctionType
ALU = mybir.AluOpType
AX = mybir.AxisListType

CORES = 8
L = 16384
D = 2048
E = 4096
N = 16
DT = 128
DC = 4
ELOC = E // CORES
TLOC = L // CORES
CL = 1024
NCH = L // CL
EPS = 1e-5
RT = 256
NRD = TLOC // RT
RINV = 1.0 / D


class PsumMgr:
    def __init__(self):
        self.rel = {}

    def acquire(self, eng, bank):
        r = self.rel.get(id(bank))
        if r is not None:
            sem, cnt = r
            eng.wait_ge(sem, cnt)

    def release(self, bank, sem, cnt):
        self.rel[id(bank)] = (sem, cnt)


class Sem:
    def __init__(self, h):
        self.h = h
        self.n = 0


def build_nc():
    nc = bass.Bass("TRN2", target_bir_lowering=False)

    def din(name, shape, dtype=F32):
        return nc.dram_tensor(name, shape, dtype, kind="ExternalInput").ap()

    def dout(name, shape, dtype=F32):
        return nc.dram_tensor(name, shape, dtype, kind="ExternalOutput").ap()

    def dint(name, shape, dtype=F32):
        return nc.dram_tensor(name, shape, dtype).ap()

    x_tok = din("x_tok", [TLOC, D])
    w_inT = din("w_inT", [D, ELOC])
    w_skipT = din("w_skipT", [D, ELOC])
    wd1T = din("wd1T", [ELOC, DT])
    wd2T = din("wd2T", [DT, ELOC])
    wbT = din("wbT", [ELOC, N])
    wcT = din("wcT", [ELOC, N])
    alog_r = din("alog_r", [128, 4 * N])
    convw_r = din("convw_r", [128, 4 * DC])
    convb_r = din("convb_r", [128, 4])
    bd2_r = din("bd2_r", [128, 4])
    wd_r = din("wd_r", [128, 4])
    nwcol_in = din("norm_w", [128, 16])
    w_outT_f = din("w_outT", [E, D])
    w_upT_f = din("w_upT", [D, D])
    w_downT_f = din("w_downT", [D, D])
    w_exp_f = din("w_exp", [D, D])
    w_gateT_f = din("w_gateT", [D, 8])
    ident_in = din("ident", [128, 128])
    out_tok = dout("out_tok", [TLOC, D])

    ag_xn_in = dint("ag_xn_in", [1, D * TLOC], BF16)
    ag_xn_out = dint("ag_xn_out", [CORES, D * TLOC], BF16)
    ag_xn_v = ag_xn_out.rearrange("r (b p f) -> (r b) p f", p=128, f=512)
    ar_in = [dint(f"ar_in{i}", [DT + 2 * N, CL]) for i in range(2)]
    ar_out = [dint(f"ar_out{i}", [DT + 2 * N, CL]) for i in range(2)]
    bc_dram = [dint(f"bc_dram{i}", [2 * N, CL], BF16) for i in range(2)]
    y_a2a_in = dint("y_a2a_in", [CORES * 4 * 128, TLOC], BF16)
    y_a2a_out = dint("y_a2a_out", [CORES * 4 * 128, TLOC], BF16)
    wout_bf = dint("wout_bf", [E, D], BF16)
    wup_bf = dint("wup_bf", [D, D], BF16)
    wdn_bf = dint("wdn_bf", [D, D], BF16)
    wexp_bf = dint("wexp_bf", [D, D], BF16)

    with contextlib.ExitStack() as top:
        TE = top.enter_context
        ident_bf = TE(nc.sbuf_tensor("ident_bf", [128, 128], BF16))
        ident_f = TE(nc.sbuf_tensor("ident_f", [128, 128], F32))
        nwcol = TE(nc.sbuf_tensor("nwcol", [128, 16], F32))
        convb_sb = TE(nc.sbuf_tensor("convb_sb", [128, 4], F32))
        bd2_sb = TE(nc.sbuf_tensor("bd2_sb", [128, 4], F32))
        wdv_sb = TE(nc.sbuf_tensor("wdv_sb", [128, 4], F32))
        convw_sb = TE(nc.sbuf_tensor("convw_sb", [128, 16], F32))
        alog_sb = TE(nc.sbuf_tensor("alog_sb", [128, 64], F32))
        a_sb = TE(nc.sbuf_tensor("a_sb", [128, 64], F32))
        carry = TE(nc.sbuf_tensor("carry", [128, 64], F32))
        gate_sb = TE(nc.sbuf_tensor("gate_sb", [128, 128], BF16))
        sm = TE(nc.sbuf_tensor("sm", [128, 32], F32))
        cst = TE(nc.sbuf_tensor("cst", [128, 2], F32))

        s_ld = Sem(TE(nc.semaphore("s_ld")))
        s_gd = Sem(TE(nc.semaphore("s_gd")))
        s_cc = Sem(TE(nc.semaphore("s_cc")))
        s_pe = Sem(TE(nc.semaphore("s_pe")))
        s_act = Sem(TE(nc.semaphore("s_act")))
        s_dve = Sem(TE(nc.semaphore("s_dve")))
        s_a = Sem(TE(nc.semaphore("s_a")))
        s_scan = Sem(TE(nc.semaphore("s_scan")))
        s_bc = Sem(TE(nc.semaphore("s_bc")))
        s_ym = Sem(TE(nc.semaphore("s_ym")))
        s_st = Sem(TE(nc.semaphore("s_st")))
        s_wc = Sem(TE(nc.semaphore("s_wc")))
        s_lp = [Sem(TE(nc.semaphore("s_la"))), Sem(TE(nc.semaphore("s_lb")))]
        s_sp = Sem(TE(nc.semaphore("s_sp")))
        s_ag = Sem(TE(nc.semaphore("s_ag")))

        def inc(c, instr, k=1):
            instr.then_inc(c.h, k)
            c.n += k
            return c.n

        pm = PsumMgr()
        block = TE(nc.Block())

        @block.sync
        def _(sync):
            for src, dst in [
                (ident_in, ident_f), (convb_r, convb_sb), (bd2_r, bd2_sb),
                (wd_r, wdv_sb), (convw_r, convw_sb), (alog_r, alog_sb),
            ]:
                inc(s_ld, sync.dma_start(out=dst[:], in_=src[:]), 16)
            inc(s_ld, sync.dma_start(out=nwcol[:], in_=nwcol_in[:]), 16)
        ld_c0 = s_ld.n

        @block.scalar
        def _(scalar):
            scalar.wait_ge(s_ld.h, ld_c0)
            inc(s_act, scalar.activation(out=a_sb[:], in_=alog_sb[:],
                                         func=AF.Exp))
        act_c0 = s_act.n

        @block.vector
        def _(vector):
            vector.wait_ge(s_ld.h, ld_c0)
            vector.tensor_copy(out=ident_bf[:], in_=ident_f[:])
            vector.memset(cst[:, 0:1], EPS)
            vector.memset(cst[:, 1:2], 1.0)
            vector.wait_ge(s_act.h, act_c0)
            vector.tensor_scalar_mul(out=a_sb[:], in0=a_sb[:], scalar1=-1.0)
            inc(s_dve, vector.memset(carry[:], 0.0))

        with contextlib.ExitStack() as st12:
            SE = st12.enter_context
            w_inT_sb = SE(nc.sbuf_tensor("w_inT_sb", [128, 16 * ELOC], BF16))
            w_skT_sb = SE(nc.sbuf_tensor("w_skT_sb", [128, 16 * ELOC], BF16))
            wd1T_sb = SE(nc.sbuf_tensor("wd1T_sb", [128, 4 * DT], BF16))
            wd2T_sb = SE(nc.sbuf_tensor("wd2T_sb", [128, ELOC], BF16))
            wbT_sb = SE(nc.sbuf_tensor("wbT_sb", [128, 4 * N], BF16))
            wcT_sb = SE(nc.sbuf_tensor("wcT_sb", [128, 4 * N], BF16))

            with contextlib.ExitStack() as csc:
                CE = csc.enter_context
                wstg = [CE(nc.sbuf_tensor(f"wstg{i}", [128, D], F32))
                        for i in range(2)]
                wstg_b = [CE(nc.sbuf_tensor(f"wstg_b{i}", [128, D], BF16))
                          for i in range(2)]
                jobs = []
                for k in range(16):
                    jobs.append((w_inT[k * 128:(k + 1) * 128, :], ELOC, "sb",
                                 w_inT_sb, k * ELOC))
                for k in range(16):
                    jobs.append((w_skipT[k * 128:(k + 1) * 128, :], ELOC, "sb",
                                 w_skT_sb, k * ELOC))
                for i in range(4):
                    jobs.append((wd1T[i * 128:(i + 1) * 128, :], DT, "sb",
                                 wd1T_sb, i * DT))
                jobs.append((wd2T[:, :], ELOC, "sb", wd2T_sb, 0))
                for i in range(4):
                    jobs.append((wbT[i * 128:(i + 1) * 128, :], N, "sb",
                                 wbT_sb, i * N))
                for i in range(4):
                    jobs.append((wcT[i * 128:(i + 1) * 128, :], N, "sb",
                                 wcT_sb, i * N))
                for k in range(16):
                    jobs.append((w_gateT_f[k * 128:(k + 1) * 128, :], 8, "sb",
                                 gate_sb, k * 8))
                for blk in range(32):
                    jobs.append((w_outT_f[blk * 128:(blk + 1) * 128, :], D,
                                 "dr", wout_bf, blk * 128))
                for wf, wb_ in [(w_upT_f, wup_bf), (w_downT_f, wdn_bf),
                                (w_exp_f, wexp_bf)]:
                    for blk in range(16):
                        jobs.append((wf[blk * 128:(blk + 1) * 128, :], D,
                                     "dr", wb_, blk * 128))

                dr_of = {}
                ndr = 0
                for idx, j in enumerate(jobs):
                    if j[2] == "dr":
                        dr_of[idx] = ndr
                        ndr += 1

                with contextlib.ExitStack() as s1s:
                    S1 = s1s.enter_context
                    xs1 = [S1(nc.sbuf_tensor(f"xs1_{i}", [128, D], F32))
                           for i in range(2)]
                    scr1 = S1(nc.sbuf_tensor("scr1", [128, D], F32))
                    ln1 = S1(nc.sbuf_tensor("ln1", [128, D], F32))
                    xn_bf = S1(nc.sbuf_tensor("xn_bf", [128, D], BF16))
                    xnT_g = S1(nc.sbuf_tensor("xnT_g", [128, 16 * 512], BF16))
                    ptp = [S1(nc.psum_tensor(f"ptp{g}", [128, 512], BF16))
                           for g in range(4)]

                    ld_b, act_b, dve_b = s_ld.n, s_act.n, s_dve.n
                    pe_b, gd_b = s_pe.n, s_gd.n

                    def act1(tt, k):
                        return act_b + 14 * tt + k

                    def dve1(tt, k):
                        return dve_b + 2 * tt + k

                    x_ld_cnt = []

                    @block.sync
                    def _(sync):
                        for tt in range(16):
                            sync.wait_ge(s_dve.h,
                                         dve1(tt - 2, 1) if tt >= 2 else 0)
                            x_ld_cnt.append(inc(
                                s_lp[tt % 2], sync.dma_start(
                                    out=xs1[tt % 2][:],
                                    in_=x_tok[tt * 128:(tt + 1) * 128, :]),
                                16))

                    @block.tensor
                    def _(tensor):
                        for tt in range(16):
                            tensor.wait_ge(s_dve.h, dve1(tt, 2))
                            if tt >= 1:
                                tensor.wait_ge(s_act.h, act1(tt - 1, 14))
                            for g in range(4):
                                for m in range(4):
                                    k = 4 * g + m
                                    ins_ = tensor.transpose(
                                        ptp[g][:, m * 128:(m + 1) * 128],
                                        xn_bf[:, k * 128:(k + 1) * 128],
                                        ident_bf[:])
                                inc(s_pe, ins_)

                    @block.scalar
                    def _(scalar):
                        scalar.wait_ge(s_dve.h, dve_b)
                        for tt in range(16):
                            if tt >= 1:
                                scalar.wait_ge(s_dve.h, dve1(tt - 1, 2))
                            scalar.wait_ge(s_lp[tt % 2].h, x_ld_cnt[tt])
                            inc(s_act, scalar.activation(
                                out=scr1[:], in_=xs1[tt % 2][:],
                                func=AF.Square, accum_out=sm[:, 1:2]))
                            inc(s_act, scalar.activation(
                                out=sm[:, 3:4], in_=sm[:, 1:2],
                                func=AF.Copy, scale=RINV, bias=EPS))
                            inc(s_act, scalar.activation(
                                out=scr1[:], in_=xs1[tt % 2][:],
                                func=AF.Copy, accum_out=sm[:, 0:1]))
                            inc(s_act, scalar.activation(
                                out=sm[:, 2:3], in_=sm[:, 0:1],
                                func=AF.Square, scale=RINV))
                            inc(s_act, scalar.activation(
                                out=sm[:, 4:5], in_=sm[:, 2:3],
                                func=AF.Ln, scale=-1.0, bias=sm[:, 3:4]))
                            inc(s_act, scalar.activation(
                                out=sm[:, 5:6], in_=sm[:, 4:5],
                                func=AF.Exp, scale=-0.5))
                            inc(s_act, scalar.activation(
                                out=sm[:, 6:7], in_=sm[:, 0:1],
                                func=AF.Copy, scale=RINV))
                            scalar.wait_ge(s_dve.h, dve1(tt, 1))
                            inc(s_act, scalar.activation(
                                out=scr1[:], in_=ln1[:], func=AF.Square,
                                accum_out=sm[:, 7:8]))
                            inc(s_act, scalar.activation(
                                out=sm[:, 8:9], in_=sm[:, 7:8],
                                func=AF.Ln, scale=RINV, bias=cst[:, 0:1]))
                            inc(s_act, scalar.activation(
                                out=sm[:, 9:10], in_=sm[:, 8:9],
                                func=AF.Exp, scale=-0.5))
                            for g in range(4):
                                scalar.wait_ge(s_pe.h, pe_b + 4 * tt + g + 1)
                                if tt >= 4:
                                    scalar.wait_ge(s_gd.h,
                                                   gd_b + 256 * (tt // 4))
                                dst = xnT_g.rearrange(
                                    "p (k f) -> p k f", k=16)[
                                    :, 4 * g:4 * g + 4,
                                    (tt % 4) * 128:(tt % 4 + 1) * 128]
                                inc(s_act, scalar.copy(
                                    out=dst,
                                    in_=ptp[g].rearrange(
                                        "p (m f) -> p m f", m=4)))

                    @block.vector
                    def _(vector):
                        for tt in range(16):
                            vector.wait_ge(s_act.h, act1(tt, 7))
                            inc(s_dve, vector.tensor_scalar(
                                out=ln1[:], in0=xs1[tt % 2][:],
                                scalar1=sm[:, 6:7], scalar2=sm[:, 5:6],
                                op0=ALU.subtract, op1=ALU.mult))
                            vector.wait_ge(s_act.h, act1(tt, 10))
                            vector.wait_ge(s_pe.h, pe_b + 4 * tt)
                            inc(s_dve, vector.tensor_scalar_mul(
                                out=xn_bf[:], in0=ln1[:],
                                scalar1=sm[:, 9:10]))

                    @block.gpsimd
                    def _(gpsimd):
                        for grp in range(4):
                            gpsimd.wait_ge(s_act.h, act1(4 * grp + 3, 14))
                            for k in range(16):
                                gk = grp * 16 + k
                                dst = ag_xn_in.rearrange(
                                    "r (b p f) -> (r b) p f", p=128, f=512)[gk]
                                inc(s_gd, gpsimd.dma_start(
                                    out=dst,
                                    in_=xnT_g[:, k * 512:(k + 1) * 512]), 16)
                        gpsimd.wait_ge(s_gd.h, s_gd.n)
                        inc(s_cc, gpsimd.collective_compute(
                            "AllGather", ALU.bypass,
                            replica_groups=[list(range(CORES))],
                            ins=[ag_xn_in.opt()], outs=[ag_xn_out.opt()]))
                    cc_agxn = s_cc.n

                ld_w0, act_w0, wc_w0, gd_w0 = s_ld.n, s_act.n, s_wc.n, s_gd.n
                cons_sem = []
                na, nw = act_w0, wc_w0
                for idx, j in enumerate(jobs):
                    if j[2] == "sb" and idx < 32:
                        na += 1
                        cons_sem.append((s_act, na))
                    else:
                        nw += 1
                        cons_sem.append((s_wc, nw))
                n_wc_sb = sum(1 for idx, j in enumerate(jobs)
                              if j[2] == "sb" and idx >= 32)

                w_ld_cnt = []

                @block.sync
                def _(sync):
                    for idx, (src, cols, kind, buf, off) in enumerate(jobs):
                        if idx >= 2:
                            sem, cnt = cons_sem[idx - 2]
                            sync.wait_ge(sem.h, cnt)
                        w_ld_cnt.append(inc(
                            s_lp[idx % 2], sync.dma_start(
                                out=wstg[idx % 2][:, 0:cols], in_=src), 16))

                @block.scalar
                def _(scalar):
                    for idx, (src, cols, kind, buf, off) in enumerate(jobs):
                        if not (kind == "sb" and idx < 32):
                            continue
                        scalar.wait_ge(s_lp[idx % 2].h, w_ld_cnt[idx])
                        inc(s_act, scalar.mul(
                            out=buf[:, off:off + cols],
                            in_=wstg[idx % 2][:, 0:cols],
                            mul=nwcol[:, (idx % 16):(idx % 16) + 1]))

                act_wsb_done = s_act.n

                @block.gpsimd
                def _(gpsimd):
                    for idx, (src, cols, kind, buf, off) in enumerate(jobs):
                        if kind == "sb" and idx < 32:
                            continue
                        gpsimd.wait_ge(s_lp[idx % 2].h, w_ld_cnt[idx])
                        if kind == "sb":
                            inc(s_wc, gpsimd.tensor_copy(
                                out=buf[:, off:off + cols],
                                in_=wstg[idx % 2][:, 0:cols]))
                        else:
                            d = dr_of[idx]
                            if d >= 2:
                                gpsimd.wait_ge(s_gd.h, gd_w0 + 16 * (d - 1))
                            inc(s_wc, gpsimd.tensor_copy(
                                out=wstg_b[d % 2][:, 0:cols],
                                in_=wstg[idx % 2][:, 0:cols]))
                            inc(s_gd, gpsimd.dma_start(
                                out=buf[off:off + 128, :],
                                in_=wstg_b[d % 2][:, 0:cols]), 16)
                wc_sb_done = wc_w0 + n_wc_sb
                wc_all_done = s_wc.n
                gd_dr_done = s_gd.n

            with contextlib.ExitStack() as s2s:
                S2 = s2s.enter_context
                xn_rhs = S2(nc.sbuf_tensor("xn_rhs", [128, 16 * 512], BF16))
                u_proj = [S2(nc.sbuf_tensor(f"u_proj{i}", [128, CL + 4], F32))
                          for i in range(4)]
                ss_sb = [S2(nc.sbuf_tensor(f"ss{i}", [128, CL], BF16))
                         for i in range(4)]
                u_bf = [S2(nc.sbuf_tensor(f"u_bf{i}", [128, CL], BF16))
                        for i in range(4)]
                delta = [S2(nc.sbuf_tensor(f"delta{i}", [128, CL], F32))
                         for i in range(4)]
                du_bf = [S2(nc.sbuf_tensor(f"du_bf{i}", [128, CL], BF16))
                         for i in range(4)]
                d1_sb = S2(nc.sbuf_tensor("d1_sb", [128, CL], F32))
                d1_bf = S2(nc.sbuf_tensor("d1_bf", [128, CL], BF16))
                bp_sb = S2(nc.sbuf_tensor("bp_sb", [16, CL], F32))
                cp_sb = S2(nc.sbuf_tensor("cp_sb", [16, CL], F32))
                bcb_sb = S2(nc.sbuf_tensor("bcb_sb", [16, CL], BF16))
                ccb_sb = S2(nc.sbuf_tensor("ccb_sb", [16, CL], BF16))
                uc = [S2(nc.sbuf_tensor(f"uc{i}", [128, CL], F32))
                      for i in range(2)]
                zbf = [S2(nc.sbuf_tensor(f"zbf{i}", [128, 512], F32))
                       for i in range(2)]
                zcf = [S2(nc.sbuf_tensor(f"zcf{i}", [128, 512], F32))
                       for i in range(2)]
                B_bc = [S2(nc.sbuf_tensor(f"B_bc{i}", [128, CL], BF16))
                        for i in range(3)]
                C_bc = [S2(nc.sbuf_tensor(f"C_bc{i}", [128, CL], BF16))
                        for i in range(3)]
                a_bf = [S2(nc.sbuf_tensor(f"a_bf{i}", [128, CL], BF16))
                        for i in range(2)]
                b_bf = S2(nc.sbuf_tensor("b_bf", [128, CL], BF16))
                h_d = [S2(nc.sbuf_tensor(f"h_d{i}", [128, CL], BF16))
                       for i in range(2)]
                t_f = S2(nc.sbuf_tensor("t_f", [128, CL], F32))
                t_p = S2(nc.sbuf_tensor("t_p", [128, CL], F32))
                y_acc = [S2(nc.sbuf_tensor(f"y_acc{i}", [128, CL], F32))
                         for i in range(4)]
                ym_bf = [S2(nc.sbuf_tensor(f"ym_bf{i}", [128, CL], BF16))
                         for i in range(2)]
                pU = [S2(nc.psum_tensor(f"pU{i}", [128, 512], F32))
                      for i in range(2)]
                pS = [S2(nc.psum_tensor(f"pS{i}", [128, 512], F32))
                      for i in range(2)]
                pD1 = S2(nc.psum_tensor("pD1", [128, 512], F32))
                pB = S2(nc.psum_tensor("pB", [16, 512], F32))
                pC = S2(nc.psum_tensor("pC", [16, 512], F32))
                pD2 = S2(nc.psum_tensor("pD2", [128, 512], F32))

                for ch in range(NCH):
                    jj, half = ch // 2, ch % 2

                    @block.vector
                    def _(vector, ch=ch):
                        if ch == 0:
                            vector.wait_ge(s_wc.h, wc_all_done)
                            vector.wait_ge(s_gd.h, gd_dr_done)
                        for i in range(4):
                            if ch == 0:
                                ins_ = vector.memset(u_proj[i][:, 0:3], 0.0)
                            else:
                                ins_ = vector.tensor_copy(
                                    out=u_proj[i][:, 0:3],
                                    in_=u_proj[i][:, CL:CL + 3])
                        inc(s_dve, ins_)
                    halo_cnt = s_dve.n

                    ld_ch, pe_ch = s_ld.n, s_pe.n

                    @block.sync
                    def _(sync, ch=ch, ld_ch=ld_ch, pe_ch=pe_ch):
                        if ch == 0:
                            sync.wait_ge(s_cc.h, cc_agxn)
                            sync.wait_ge(s_wc.h, wc_all_done)
                            sync.wait_ge(s_gd.h, gd_dr_done)
                        else:
                            sync.wait_ge(s_ym.h, 4 * ch)
                        for sub in range(2):
                            if sub == 1:
                                sync.wait_ge(s_pe.h, pe_ch + 8)
                            g0 = ch * 2 + sub
                            oc, gg = g0 // 4, g0 % 4
                            for k in range(16):
                                inc(s_ld, sync.dma_start(
                                    out=xn_rhs[:, k * 512:(k + 1) * 512],
                                    in_=ag_xn_v[oc * 64 + gg * 16 + k]), 16)

                    act_ev0 = s_act.n
                    grp_list = []
                    for sub in range(2):
                        for i in range(4):
                            grp_list.append(("u", i, sub))
                        for i in range(4):
                            grp_list.append(("sk", i, sub))

                    @block.tensor
                    def _(tensor, ch=ch, ld_ch=ld_ch, act_ev0=act_ev0):
                        if ch == 0:
                            tensor.wait_ge(s_act.h, act_wsb_done)
                            tensor.wait_ge(s_wc.h, wc_sb_done)
                        for g, (kind, i, sub) in enumerate(grp_list):
                            tensor.wait_ge(s_ld.h, ld_ch + 256 * (sub + 1))
                            bank = (pU if kind == "u" else pS)[i % 2]
                            pm.acquire(tensor, bank)
                            wsb = w_inT_sb if kind == "u" else w_skT_sb
                            for k in range(16):
                                ins_ = tensor.matmul(
                                    bank[:],
                                    wsb[:, k * ELOC + i * 128:
                                        k * ELOC + (i + 1) * 128],
                                    xn_rhs[:, k * 512:(k + 1) * 512],
                                    start=(k == 0), stop=(k == 15))
                            inc(s_pe, ins_)
                            pm.release(bank, s_act.h, act_ev0 + g + 1)

                    @block.scalar
                    def _(scalar, pe_ch=pe_ch, halo_cnt=halo_cnt):
                        scalar.wait_ge(s_dve.h, halo_cnt)
                        for g, (kind, i, sub) in enumerate(grp_list):
                            scalar.wait_ge(s_pe.h, pe_ch + g + 1)
                            bank = (pU if kind == "u" else pS)[i % 2]
                            if kind == "u":
                                inc(s_act, scalar.copy(
                                    out=u_proj[i][:, 3 + sub * 512:
                                                  3 + (sub + 1) * 512],
                                    in_=bank[:]))
                            else:
                                inc(s_act, scalar.activation(
                                    out=ss_sb[i][:, sub * 512:(sub + 1) * 512],
                                    in_=bank[:], func=AF.Silu))

                    conv0 = s_dve.n
                    silu_b = s_act.n

                    @block.vector
                    def _(vector, act_ev0=act_ev0, silu_b=silu_b):
                        for i in range(4):
                            vector.wait_ge(s_act.h, act_ev0 + 8 + i + 1)
                            if i >= 1:
                                vector.wait_ge(s_act.h, silu_b + i)
                            vector.tensor_scalar_mul(
                                out=uc[0][:], in0=u_proj[i][:, 0:CL],
                                scalar1=convw_sb[:, 4 * i:4 * i + 1])
                            vector.scalar_tensor_tensor(
                                out=uc[1][:], in0=u_proj[i][:, 1:1 + CL],
                                scalar=convw_sb[:, 4 * i + 1:4 * i + 2],
                                in1=uc[0][:], op0=ALU.mult, op1=ALU.add)
                            vector.scalar_tensor_tensor(
                                out=uc[0][:], in0=u_proj[i][:, 2:2 + CL],
                                scalar=convw_sb[:, 4 * i + 2:4 * i + 3],
                                in1=uc[1][:], op0=ALU.mult, op1=ALU.add)
                            inc(s_dve, vector.scalar_tensor_tensor(
                                out=uc[1][:], in0=u_proj[i][:, 3:3 + CL],
                                scalar=convw_sb[:, 4 * i + 3:4 * i + 4],
                                in1=uc[0][:], op0=ALU.mult, op1=ALU.add))

                    @block.scalar
                    def _(scalar, conv0=conv0):
                        for i in range(4):
                            scalar.wait_ge(s_dve.h, conv0 + i + 1)
                            inc(s_act, scalar.activation(
                                out=u_bf[i][:], in_=uc[1][:], func=AF.Silu,
                                bias=convb_sb[:, i:i + 1]))
                    silu_end = s_act.n

                    d1bc = [("d1", 0), ("B", 0), ("C", 0),
                            ("d1", 1), ("B", 1), ("C", 1)]
                    pe_d1b = s_pe.n
                    act_ev1 = s_act.n

                    @block.tensor
                    def _(tensor, ch=ch, silu_end=silu_end, act_ev1=act_ev1):
                        if ch == 0:
                            tensor.wait_ge(s_wc.h, wc_sb_done)
                        tensor.wait_ge(s_act.h, silu_end)
                        for g, (kind, sub) in enumerate(d1bc):
                            bank = {"d1": pD1, "B": pB, "C": pC}[kind]
                            wsb = {"d1": wd1T_sb, "B": wbT_sb,
                                   "C": wcT_sb}[kind]
                            w_n = {"d1": DT, "B": N, "C": N}[kind]
                            pm.acquire(tensor, bank)
                            for i in range(4):
                                ins_ = tensor.matmul(
                                    bank[:],
                                    wsb[:, i * w_n:(i + 1) * w_n],
                                    u_bf[i][:, sub * 512:(sub + 1) * 512],
                                    start=(i == 0), stop=(i == 3))
                            inc(s_pe, ins_)
                            pm.release(bank, s_act.h, act_ev1 + g + 1)

                    @block.scalar
                    def _(scalar, pe_d1b=pe_d1b):
                        for g, (kind, sub) in enumerate(d1bc):
                            scalar.wait_ge(s_pe.h, pe_d1b + g + 1)
                            c0, c1 = sub * 512, (sub + 1) * 512
                            if kind == "d1":
                                inc(s_act, scalar.copy(out=d1_sb[:, c0:c1],
                                                       in_=pD1[:]))
                            elif kind == "B":
                                inc(s_act, scalar.copy(out=bp_sb[:, c0:c1],
                                                       in_=pB[:]))
                            else:
                                inc(s_act, scalar.copy(out=cp_sb[:, c0:c1],
                                                       in_=pC[:]))
                    evd1_end = s_act.n

                    @block.gpsimd
                    def _(gpsimd, ch=ch, evd1_end=evd1_end):
                        gpsimd.wait_ge(s_act.h, evd1_end)
                        gpsimd.wait_ge(s_gd.h, s_gd.n)
                        if ch >= 2:
                            gpsimd.wait_ge(s_cc.h, s_cc.n)
                        inc(s_gd, gpsimd.dma_start(
                            out=ar_in[ch % 2][0:DT, :], in_=d1_sb[:]), 16)
                        inc(s_gd, gpsimd.dma_start(
                            out=ar_in[ch % 2][DT:DT + N, :], in_=bp_sb[:]), 16)
                        inc(s_gd, gpsimd.dma_start(
                            out=ar_in[ch % 2][DT + N:DT + 2 * N, :],
                            in_=cp_sb[:]), 16)
                        gpsimd.wait_ge(s_gd.h, s_gd.n)
                        inc(s_cc, gpsimd.collective_compute(
                            "AllReduce", ALU.add,
                            replica_groups=[list(range(CORES))],
                            ins=[ar_in[ch % 2].opt()],
                            outs=[ar_out[ch % 2].opt()]))
                        gpsimd.wait_ge(s_cc.h, s_cc.n)
                        inc(s_gd, gpsimd.dma_start(
                            out=d1_sb[:], in_=ar_out[ch % 2][0:DT, :]), 16)
                        inc(s_gd, gpsimd.dma_start(
                            out=bp_sb[:],
                            in_=ar_out[ch % 2][DT:DT + N, :]), 16)
                        inc(s_gd, gpsimd.dma_start(
                            out=cp_sb[:],
                            in_=ar_out[ch % 2][DT + N:DT + 2 * N, :]), 16)
                    gd_arl = s_gd.n

                    @block.vector
                    def _(vector, gd_arl=gd_arl):
                        vector.wait_ge(s_gd.h, gd_arl)
                        vector.tensor_copy(out=d1_bf[:], in_=d1_sb[:])
                        vector.tensor_copy(out=bcb_sb[:], in_=bp_sb[:])
                        inc(s_dve, vector.tensor_copy(out=ccb_sb[:],
                                                      in_=cp_sb[:]))
                    cast_cnt = s_dve.n

                    @block.gpsimd
                    def _(gpsimd, ch=ch, cast_cnt=cast_cnt):
                        gpsimd.wait_ge(s_dve.h, cast_cnt)
                        inc(s_gd, gpsimd.dma_start(
                            out=bc_dram[ch % 2][0:N, :], in_=bcb_sb[:]), 16)
                        inc(s_gd, gpsimd.dma_start(
                            out=bc_dram[ch % 2][N:2 * N, :], in_=ccb_sb[:]),
                            16)
                    gd_bcd = s_gd.n

                    dve_sp0 = s_dve.n
                    act_sp0 = s_act.n

                    def zc_cnt(g, base=dve_sp0):
                        return base + (1 if g == 0 else 2 * g)

                    def mx_cnt(g, base=dve_sp0):
                        return base + (2 * g + 3 if g <= 6 else 16)

                    def du_cnt(i, base=dve_sp0):
                        return base + 16 + i + 1

                    d2banks = [pU[0], pU[1], pS[0], pS[1], pD1, pD2,
                               pU[0], pU[1]]
                    pe_d2 = s_pe.n

                    @block.tensor
                    def _(tensor, ch=ch, cast_cnt=cast_cnt, dve_sp0=dve_sp0):
                        if ch == 0:
                            tensor.wait_ge(s_wc.h, wc_sb_done)
                        tensor.wait_ge(s_dve.h, cast_cnt)
                        for g in range(8):
                            i, sub = g // 2, g % 2
                            bank = d2banks[g]
                            pm.acquire(tensor, bank)
                            ins_ = tensor.matmul(
                                bank[:],
                                wd2T_sb[:, i * 128:(i + 1) * 128],
                                d1_bf[:, sub * 512:(sub + 1) * 512],
                                start=True, stop=True)
                            inc(s_pe, ins_)
                            pm.release(bank, s_dve.h, zc_cnt(g))

                    @block.vector
                    def _(vector, pe_d2=pe_d2, act_sp0=act_sp0,
                          dve_sp0=dve_sp0):
                        for g in range(8):
                            i = g // 2
                            bank = d2banks[g]
                            vector.wait_ge(s_pe.h, pe_d2 + g + 1)
                            if g >= 2:
                                vector.wait_ge(s_act.h, act_sp0 + g - 1)
                            vector.tensor_scalar(
                                out=zbf[g % 2][:], in0=bank[:],
                                scalar1=bd2_sb[:, i:i + 1], scalar2=None,
                                op0=ALU.add)
                            inc(s_dve, vector.tensor_scalar_min(
                                out=zcf[g % 2][:], in0=zbf[g % 2][:],
                                scalar1=80.0))
                            if g >= 1:
                                gp = g - 1
                                ip, sp_ = gp // 2, gp % 2
                                vector.wait_ge(s_act.h, act_sp0 + gp + 1)
                                inc(s_dve, vector.tensor_max(
                                    out=delta[ip][:,
                                                  sp_ * 512:(sp_ + 1) * 512],
                                    in0=zbf[gp % 2][:], in1=zcf[gp % 2][:]))
                        vector.wait_ge(s_act.h, act_sp0 + 8)
                        inc(s_dve, vector.tensor_max(
                            out=delta[3][:, 512:1024],
                            in0=zbf[1][:], in1=zcf[1][:]))
                        for i in range(4):
                            vector.tensor_copy(out=b_bf[:], in_=delta[i][:])
                            inc(s_dve, vector.tensor_mul(
                                out=du_bf[i][:], in0=b_bf[:], in1=u_bf[i][:]))

                    @block.scalar
                    def _(scalar, dve_sp0=dve_sp0):
                        for g in range(8):
                            scalar.wait_ge(s_dve.h, zc_cnt(g, dve_sp0))
                            scalar.activation(out=uc[g % 2][:, 0:512],
                                              in_=zcf[g % 2][:], func=AF.Exp)
                            inc(s_act, scalar.activation(
                                out=zcf[g % 2][:], in_=uc[g % 2][:, 0:512],
                                func=AF.Ln, bias=cst[:, 1:2]))

                    a_b = s_a.n
                    scan_b = s_scan.n

                    @block.scalar
                    def _(scalar, a_b=a_b, scan_b=scan_b, dve_sp0=dve_sp0):
                        for idx in range(64):
                            n_, i = idx // 4, idx % 4
                            scalar.wait_ge(s_dve.h, du_cnt(i, dve_sp0))
                            if idx >= 2:
                                scalar.wait_ge(s_scan.h, scan_b + idx - 1)
                            inc(s_a, scalar.activation(
                                out=a_bf[idx % 2][:], in_=delta[i][:],
                                func=AF.Exp,
                                scale=a_sb[:, i * N + n_:i * N + n_ + 1]))

                    sp_b = s_sp.n
                    bc_b = s_bc.n

                    @block.vector
                    def _(vector, a_b=a_b, bc_b=bc_b, sp_b=sp_b,
                          scan_b=scan_b):
                        def carry_cp(j):
                            ii, nn = j % 4, j // 4
                            vector.tensor_copy(
                                out=carry[:, ii * N + nn:ii * N + nn + 1],
                                in_=h_d[j % 2][:, CL - 1:CL])

                        for idx in range(64):
                            n_, i = idx // 4, idx % 4
                            vector.wait_ge(s_a.h, a_b + idx + 1)
                            if idx == 0:
                                vector.wait_ge(s_sp.h, sp_b)
                            if i == 0:
                                vector.wait_ge(s_bc.h, bc_b + 32 * (n_ + 1))
                            if idx >= 2:
                                vector.wait_ge(s_sp.h, sp_b + idx - 1)
                            vector.tensor_mul(out=b_bf[:], in0=du_bf[i][:],
                                              in1=B_bc[n_ % 3][:])
                            inc(s_scan, vector.tensor_tensor_scan(
                                out=h_d[idx % 2][:], data0=a_bf[idx % 2][:],
                                data1=b_bf[:],
                                initial=carry[:, i * N + n_:i * N + n_ + 1],
                                op0=ALU.mult, op1=ALU.add))
                            if idx >= 1:
                                carry_cp(idx - 1)
                        vector.memset(b_bf[:, 0:8], 0.0)
                        vector.memset(b_bf[:, 8:16], 0.0)
                        carry_cp(63)

                    @block.gpsimd
                    def _(gpsimd, ch=ch, gd_bcd=gd_bcd, scan_b=scan_b,
                          sp_b=sp_b, bc_b=bc_b):
                        def bcast(n_):
                            inc(s_bc, gpsimd.dma_start(
                                out=B_bc[n_ % 3][:],
                                in_=bc_dram[ch % 2][n_:n_ + 1, :]
                                .broadcast_to((128, CL))), 16)
                            inc(s_bc, gpsimd.dma_start(
                                out=C_bc[n_ % 3][:],
                                in_=bc_dram[ch % 2][N + n_:N + n_ + 1, :]
                                .broadcast_to((128, CL))), 16)

                        gpsimd.wait_ge(s_gd.h, gd_bcd)
                        gpsimd.wait_ge(s_scan.h, scan_b)
                        for n_ in range(3):
                            bcast(n_)
                        for idx in range(64):
                            n_, i = idx // 4, idx % 4
                            gpsimd.wait_ge(s_scan.h, scan_b + idx + 1)
                            if i == 0:
                                gpsimd.wait_ge(s_bc.h, bc_b + 32 * (n_ + 1))
                            gpsimd.tensor_mul(out=t_p[:],
                                              in0=h_d[idx % 2][:],
                                              in1=C_bc[n_ % 3][:])
                            if n_ == 0:
                                ins_ = gpsimd.tensor_copy(out=y_acc[i][:],
                                                          in_=t_p[:])
                            else:
                                ins_ = gpsimd.tensor_add(
                                    out=y_acc[i][:], in0=y_acc[i][:],
                                    in1=t_p[:])
                            inc(s_sp, ins_)
                            if i == 3 and n_ + 3 < 16:
                                bcast(n_ + 3)

                    gd_ym0 = s_gd.n

                    @block.vector
                    def _(vector, gd_ym0=gd_ym0, sp_b=sp_b):
                        vector.wait_ge(s_sp.h, sp_b + 64)
                        vector.wait_ge(s_gd.h, gd_ym0)
                        for i in range(4):
                            if i >= 2:
                                vector.wait_ge(s_gd.h, gd_ym0 + 16 * (i - 1))
                            vector.scalar_tensor_tensor(
                                out=t_f[:], in0=u_bf[i][:],
                                scalar=wdv_sb[:, i:i + 1], in1=y_acc[i][:],
                                op0=ALU.mult, op1=ALU.add)
                            vector.tensor_copy(out=b_bf[:], in_=t_f[:])
                            inc(s_ym, vector.tensor_mul(
                                out=ym_bf[i % 2][:], in0=b_bf[:],
                                in1=ss_sb[i][:]))

                    ym_base = s_ym.n - 4

                    @block.gpsimd
                    def _(gpsimd, jj=jj, half=half, ym_base=ym_base):
                        for i in range(4):
                            gpsimd.wait_ge(s_ym.h, ym_base + i + 1)
                            r0 = (jj * 4 + i) * 128
                            inc(s_gd, gpsimd.dma_start(
                                out=y_a2a_in[r0:r0 + 128,
                                             half * CL:(half + 1) * CL],
                                in_=ym_bf[i % 2][:]), 16)

                @block.gpsimd
                def _(gpsimd):
                    gpsimd.wait_ge(s_gd.h, s_gd.n)
                    with allow_ici_all_to_all():
                        inc(s_cc, gpsimd.collective_compute(
                            "AllToAll", ALU.bypass,
                            replica_groups=[list(range(CORES))],
                            ins=[y_a2a_in.opt()], outs=[y_a2a_out.opt()]))
                cc_y = s_cc.n

        bar_act, bar_dve = s_act.n, s_dve.n

        with contextlib.ExitStack() as s3s:
            S3 = s3s.enter_context
            y_sb = S3(nc.sbuf_tensor("y_sb", [128, 32 * RT], BF16))
            wch = [S3(nc.sbuf_tensor(f"wch{i}", [128, D], BF16))
                   for i in range(2)]
            fm = S3(nc.sbuf_tensor("fm", [128, 16 * RT], F32))
            hid = S3(nc.sbuf_tensor("hid", [128, 16 * RT], BF16))
            xn2_fm = S3(nc.sbuf_tensor("xn2_fm", [128, 16 * RT], BF16))
            xn2s_fm = S3(nc.sbuf_tensor("xn2s_fm", [128, 16 * RT], BF16))
            xs3 = [S3(nc.sbuf_tensor(f"xs3_{i}", [128, D], F32))
                   for i in range(2)]
            x2t = [S3(nc.sbuf_tensor(f"x2t{i}", [128, D], F32))
                   for i in range(2)]
            xn2t = [S3(nc.sbuf_tensor(f"xn2t{i}", [128, D], BF16))
                    for i in range(2)]
            ao_tok = S3(nc.sbuf_tensor("ao_tok", [128, D], F32))
            scr3 = S3(nc.sbuf_tensor("scr3", [128, D], F32))
            lnt = S3(nc.sbuf_tensor("lnt", [128, D], F32))
            out_t = S3(nc.sbuf_tensor("out_t", [128, D], F32))
            glog = S3(nc.sbuf_tensor("glog", [8, RT], F32))
            ltok = S3(nc.sbuf_tensor("ltok", [128, 16], F32))
            jnk3 = S3(nc.sbuf_tensor("jnk3", [128, 8], F32))
            sgate = [S3(nc.sbuf_tensor(f"sgate{i}", [128, 1], F32))
                     for i in range(2)]
            pMM = [S3(nc.psum_tensor(f"pMM{i}", [128, 512], F32))
                   for i in range(8)]

            for rnd in range(NRD):
                g0 = rnd * RT
                ld_r0, pe_r0, dve_r0 = s_ld.n, s_pe.n, s_dve.n

                wch_cnt = []

                @block.sync
                def _(sync, rnd=rnd, g0=g0, ld_r0=ld_r0, pe_r0=pe_r0,
                      dve_r0=dve_r0):
                    if rnd == 0:
                        sync.wait_ge(s_cc.h, cc_y)
                        sync.wait_ge(s_dve.h, bar_dve)
                        sync.wait_ge(s_act.h, bar_act)
                    else:
                        sync.wait_ge(s_pe.h, pe_r0)
                        sync.wait_ge(s_dve.h, dve_r0)
                    for et in range(32):
                        inc(s_ld, sync.dma_start(
                            out=y_sb[:, et * RT:(et + 1) * RT],
                            in_=y_a2a_out[et * 128:(et + 1) * 128,
                                          g0:g0 + RT]), 16)
                    for tt in range(2):
                        inc(s_ld, sync.dma_start(
                            out=xs3[tt][:],
                            in_=x_tok[g0 + tt * 128:g0 + (tt + 1) * 128, :]),
                            16)
                    for ec in range(32):
                        if ec >= 2:
                            sync.wait_ge(s_pe.h, pe_r0 + ec - 1)
                        wch_cnt.append(inc(
                            s_lp[ec % 2], sync.dma_start(
                                out=wch[ec % 2][:],
                                in_=wout_bf[ec * 128:(ec + 1) * 128, :]),
                            16))

                act_r0 = s_act.n

                @block.tensor
                def _(tensor, ld_r0=ld_r0, act_r0=act_r0):
                    for ec in range(32):
                        if ec == 0:
                            tensor.wait_ge(s_ld.h, ld_r0 + 16 * 34)
                            for b in range(8):
                                pm.acquire(tensor, pMM[b])
                        tensor.wait_ge(s_lp[ec % 2].h, wch_cnt[ec])
                        if ec == 0:
                            db_order = [x for b in range(8)
                                        for x in (2 * b + 1, 2 * b)]
                        else:
                            db_order = list(range(16))
                        for db in db_order:
                            ins_ = tensor.matmul(
                                pMM[db // 2][:, (db % 2) * RT:
                                             (db % 2 + 1) * RT],
                                wch[ec % 2][:, db * 128:(db + 1) * 128],
                                y_sb[:, ec * RT:(ec + 1) * RT],
                                start=(ec == 0 and db % 2 == 1),
                                stop=(ec == 31))
                        inc(s_pe, ins_)
                    for b in range(8):
                        pm.release(pMM[b], s_act.h, act_r0 + 16)
                pe_omm = s_pe.n

                @block.scalar
                def _(scalar, pe_omm=pe_omm):
                    scalar.wait_ge(s_pe.h, pe_omm)
                    for db in range(16):
                        inc(s_act, scalar.copy(
                            out=fm[:, db * RT:(db + 1) * RT],
                            in_=pMM[db // 2][:, (db % 2) * RT:
                                             (db % 2 + 1) * RT]))
                ev_omm = s_act.n

                for tt in range(2):
                    pe_t0 = s_pe.n

                    @block.tensor
                    def _(tensor, tt=tt, ev_omm=ev_omm):
                        tensor.wait_ge(s_act.h, ev_omm)
                        for g in range(4):
                            bank = pMM[g]
                            pm.acquire(tensor, bank)
                            for m in range(4):
                                k = 4 * g + m
                                ins_ = tensor.transpose(
                                    bank[:, m * 128:(m + 1) * 128],
                                    fm[:, k * RT + tt * 128:
                                       k * RT + (tt + 1) * 128],
                                    ident_f[:])
                            inc(s_pe, ins_)
                            pm.release(bank, s_act.h, s_act.n + g + 1)

                    @block.scalar
                    def _(scalar, pe_t0=pe_t0):
                        for g in range(4):
                            scalar.wait_ge(s_pe.h, pe_t0 + g + 1)
                            inc(s_act, scalar.copy(
                                out=ao_tok.rearrange(
                                    "p (k f) -> p k f", k=16)[
                                    :, 4 * g:4 * g + 4, :],
                                in_=pMM[g].rearrange(
                                    "p (m f) -> p m f", m=4)))
                    ev_tp = s_act.n

                    act_ln0, dve_ln0 = s_act.n, s_dve.n

                    @block.scalar
                    def _(scalar, tt=tt, act_ln0=act_ln0, dve_ln0=dve_ln0):
                        inc(s_act, scalar.activation(
                            out=scr3[:], in_=ao_tok[:], func=AF.Square,
                            accum_out=sm[:, 1:2]))
                        inc(s_act, scalar.activation(
                            out=sm[:, 3:4], in_=sm[:, 1:2], func=AF.Copy,
                            scale=RINV, bias=EPS))
                        inc(s_act, scalar.activation(
                            out=scr3[:], in_=ao_tok[:], func=AF.Copy,
                            accum_out=sm[:, 0:1]))
                        inc(s_act, scalar.activation(
                            out=sm[:, 2:3], in_=sm[:, 0:1], func=AF.Square,
                            scale=RINV))
                        inc(s_act, scalar.activation(
                            out=sm[:, 4:5], in_=sm[:, 2:3], func=AF.Ln,
                            scale=-1.0, bias=sm[:, 3:4]))
                        inc(s_act, scalar.activation(
                            out=sm[:, 5:6], in_=sm[:, 4:5], func=AF.Exp,
                            scale=-0.5))
                        inc(s_act, scalar.activation(
                            out=sm[:, 6:7], in_=sm[:, 0:1], func=AF.Copy,
                            scale=RINV))
                        scalar.wait_ge(s_dve.h, dve_ln0 + 1)
                        inc(s_act, scalar.activation(
                            out=scr3[:], in_=x2t[tt][:], func=AF.Square,
                            accum_out=sm[:, 8:9]))
                        inc(s_act, scalar.activation(
                            out=sm[:, 11:12], in_=sm[:, 8:9], func=AF.Copy,
                            scale=RINV, bias=EPS))
                        inc(s_act, scalar.activation(
                            out=scr3[:], in_=x2t[tt][:], func=AF.Copy,
                            accum_out=sm[:, 7:8]))
                        inc(s_act, scalar.activation(
                            out=sm[:, 10:11], in_=sm[:, 7:8], func=AF.Square,
                            scale=RINV))
                        inc(s_act, scalar.activation(
                            out=sm[:, 12:13], in_=sm[:, 10:11], func=AF.Ln,
                            scale=-1.0, bias=sm[:, 11:12]))
                        inc(s_act, scalar.activation(
                            out=sm[:, 13:14], in_=sm[:, 12:13], func=AF.Exp,
                            scale=-0.5))
                        inc(s_act, scalar.activation(
                            out=sm[:, 14:15], in_=sm[:, 7:8], func=AF.Copy,
                            scale=RINV))

                    @block.vector
                    def _(vector, tt=tt, act_ln0=act_ln0, ev_tp=ev_tp,
                          ld_r0=ld_r0):
                        vector.wait_ge(s_act.h, ev_tp)
                        vector.wait_ge(s_ld.h, ld_r0 + 16 * 34)
                        vector.wait_ge(s_act.h, act_ln0 + 7)
                        vector.tensor_scalar(
                            out=lnt[:], in0=ao_tok[:], scalar1=sm[:, 6:7],
                            scalar2=sm[:, 5:6], op0=ALU.subtract,
                            op1=ALU.mult)
                        inc(s_dve, vector.tensor_add(
                            out=x2t[tt][:], in0=xs3[tt][:], in1=lnt[:]))
                        vector.wait_ge(s_act.h, act_ln0 + 14)
                        inc(s_dve, vector.tensor_scalar(
                            out=xn2t[tt][:], in0=x2t[tt][:],
                            scalar1=sm[:, 14:15], scalar2=sm[:, 13:14],
                            op0=ALU.subtract, op1=ALU.mult))
                    xn2_done = s_dve.n

                    pe_x0 = s_pe.n

                    @block.tensor
                    def _(tensor, tt=tt, xn2_done=xn2_done):
                        tensor.wait_ge(s_dve.h, xn2_done)
                        for g in range(4):
                            bank = pMM[4 + g]
                            pm.acquire(tensor, bank)
                            bb = bank.bitcast(BF16)
                            for m in range(4):
                                k = 4 * g + m
                                ins_ = tensor.transpose(
                                    bb[:, m * 128:(m + 1) * 128],
                                    xn2t[tt][:, k * 128:(k + 1) * 128],
                                    ident_bf[:])
                            inc(s_pe, ins_)
                            pm.release(bank, s_act.h, s_act.n + g + 1)

                    @block.scalar
                    def _(scalar, tt=tt, pe_x0=pe_x0):
                        for g in range(4):
                            scalar.wait_ge(s_pe.h, pe_x0 + g + 1)
                            dst = xn2_fm.rearrange(
                                "p (k f) -> p k f", k=16)[
                                :, 4 * g:4 * g + 4,
                                tt * 128:(tt + 1) * 128]
                            src = pMM[4 + g].bitcast(BF16)[:, 0:512]\
                                .rearrange("p (m f) -> p m f", m=4)
                            inc(s_act, scalar.copy(out=dst, in_=src))
                    ev_x = s_act.n

                    pe_g0 = s_pe.n

                    @block.tensor
                    def _(tensor, tt=tt, ev_x=ev_x):
                        tensor.wait_ge(s_act.h, ev_x)
                        bank = pMM[0]
                        pm.acquire(tensor, bank)
                        for k in range(16):
                            ins_ = tensor.matmul(
                                bank[0:8, 0:128],
                                gate_sb[:, k * 8:(k + 1) * 8],
                                xn2_fm[:, k * RT + tt * 128:
                                       k * RT + (tt + 1) * 128],
                                start=(k == 0), stop=(k == 15))
                        inc(s_pe, ins_)
                        pm.release(bank, s_act.h, s_act.n + 1)

                    @block.scalar
                    def _(scalar, tt=tt, pe_g0=pe_g0):
                        scalar.wait_ge(s_pe.h, pe_g0 + 1)
                        inc(s_act, scalar.copy(
                            out=glog[:, tt * 128:(tt + 1) * 128],
                            in_=pMM[0][0:8, 0:128]))
                    ev_g = s_act.n

                    pe_g1 = s_pe.n

                    @block.tensor
                    def _(tensor, tt=tt, ev_g=ev_g):
                        tensor.wait_ge(s_act.h, ev_g)
                        bank = pMM[1]
                        pm.acquire(tensor, bank)
                        ins_ = tensor.transpose(
                            bank[:, 0:8],
                            glog[:, tt * 128:(tt + 1) * 128],
                            ident_f[0:8, 0:8])
                        inc(s_pe, ins_)
                        pm.release(bank, s_act.h, s_act.n + 1)

                    act_s0, dve_s0 = s_act.n, s_dve.n

                    @block.scalar
                    def _(scalar, pe_g1=pe_g1, dve_s0=dve_s0):
                        scalar.wait_ge(s_pe.h, pe_g1 + 1)
                        inc(s_act, scalar.copy(out=ltok[:, 0:8],
                                               in_=pMM[1][:, 0:8]))
                        scalar.wait_ge(s_dve.h, dve_s0 + 15)
                        inc(s_act, scalar.activation(
                            out=sm[:, 19:20], in_=sm[:, 18:19], func=AF.Exp))
                        scalar.activation(out=sm[:, 20:21], in_=sm[:, 19:20],
                                          func=AF.Ln, bias=cst[:, 1:2])
                        inc(s_act, scalar.activation(
                            out=sm[:, 21:22], in_=sm[:, 20:21], func=AF.Exp,
                            scale=-1.0))

                    @block.vector
                    def _(vector, tt=tt, act_s0=act_s0):
                        def j():
                            inc(s_dve, vector.memset(jnk3[:, 0:1], 0.0))

                        vector.wait_ge(s_act.h, act_s0 + 1)
                        inc(s_dve, vector.reduce_max(
                            out=sm[:, 16:17], in_=ltok[:, 0:8], axis=AX.X))
                        j(); j()
                        inc(s_dve, vector.tensor_scalar(
                            out=ltok[:, 8:16], in0=ltok[:, 0:8],
                            scalar1=sm[:, 16:17], scalar2=None,
                            op0=ALU.is_equal))
                        j(); j()
                        inc(s_dve, vector.scalar_tensor_tensor(
                            out=ltok[:, 8:16], in0=ltok[:, 8:16],
                            scalar=-1e30, in1=ltok[:, 0:8],
                            op0=ALU.mult, op1=ALU.add))
                        j(); j()
                        inc(s_dve, vector.reduce_max(
                            out=sm[:, 17:18], in_=ltok[:, 8:16], axis=AX.X))
                        j(); j()
                        inc(s_dve, vector.tensor_sub(
                            out=sm[:, 18:19], in0=sm[:, 17:18],
                            in1=sm[:, 16:17]))
                        j(); j()
                        vector.wait_ge(s_act.h, act_s0 + 3)
                        inc(s_dve, vector.tensor_mul(
                            out=sm[:, 22:23], in0=sm[:, 19:20],
                            in1=sm[:, 21:22]))
                        j(); j()
                        inc(s_dve, vector.tensor_add(
                            out=sgate[tt][:], in0=sm[:, 21:22],
                            in1=sm[:, 22:23]))
                        j(); j()
                        inc(s_dve, vector.tensor_scalar_mul(
                            out=xn2t[tt][:], in0=xn2t[tt][:],
                            scalar1=sgate[tt][:, 0:1]))
                    s_done = s_dve.n

                    pe_xs = s_pe.n

                    @block.tensor
                    def _(tensor, tt=tt, s_done=s_done):
                        tensor.wait_ge(s_dve.h, s_done)
                        for g in range(4):
                            bank = pMM[4 + g]
                            pm.acquire(tensor, bank)
                            bb = bank.bitcast(BF16)
                            for m in range(4):
                                k = 4 * g + m
                                ins_ = tensor.transpose(
                                    bb[:, m * 128:(m + 1) * 128],
                                    xn2t[tt][:, k * 128:(k + 1) * 128],
                                    ident_bf[:])
                            inc(s_pe, ins_)
                            pm.release(bank, s_act.h, s_act.n + g + 1)

                    @block.scalar
                    def _(scalar, tt=tt, pe_xs=pe_xs):
                        for g in range(4):
                            scalar.wait_ge(s_pe.h, pe_xs + g + 1)
                            dst = xn2s_fm.rearrange(
                                "p (k f) -> p k f", k=16)[
                                :, 4 * g:4 * g + 4,
                                tt * 128:(tt + 1) * 128]
                            src = pMM[4 + g].bitcast(BF16)[:, 0:512]\
                                .rearrange("p (m f) -> p m f", m=4)
                            inc(s_act, scalar.copy(out=dst, in_=src))
                xn_fm_done = s_act.n

                ld_up0, pe_up0 = s_ld.n, s_pe.n

                wup_cnt = []

                @block.sync
                def _(sync, pe_up0=pe_up0):
                    for dc in range(16):
                        if dc < 2:
                            sync.wait_ge(s_pe.h, pe_up0)
                        else:
                            sync.wait_ge(s_pe.h, pe_up0 + dc - 1)
                        wup_cnt.append(inc(
                            s_lp[dc % 2], sync.dma_start(
                                out=wch[dc % 2][:],
                                in_=wup_bf[dc * 128:(dc + 1) * 128, :]), 16))

                act_up0 = s_act.n

                @block.tensor
                def _(tensor, ld_up0=ld_up0, xn_fm_done=xn_fm_done,
                      act_up0=act_up0):
                    tensor.wait_ge(s_act.h, xn_fm_done)
                    for dc in range(16):
                        tensor.wait_ge(s_lp[dc % 2].h, wup_cnt[dc])
                        if dc == 0:
                            for b in range(8):
                                pm.acquire(tensor, pMM[b])
                            ob_order = [x for b in range(8)
                                        for x in (2 * b + 1, 2 * b)]
                        else:
                            ob_order = list(range(16))
                        for ob in ob_order:
                            ins_ = tensor.matmul(
                                pMM[ob // 2][:, (ob % 2) * RT:
                                             (ob % 2 + 1) * RT],
                                wch[dc % 2][:, ob * 128:(ob + 1) * 128],
                                xn2_fm[:, dc * RT:(dc + 1) * RT],
                                start=(dc == 0 and ob % 2 == 1),
                                stop=(dc == 15))
                        inc(s_pe, ins_)
                    for b in range(8):
                        pm.release(pMM[b], s_act.h, act_up0 + 16)
                pe_up_end = s_pe.n

                @block.scalar
                def _(scalar, pe_up_end=pe_up_end):
                    scalar.wait_ge(s_pe.h, pe_up_end)
                    for ob in range(16):
                        inc(s_act, scalar.activation(
                            out=hid[:, ob * RT:(ob + 1) * RT],
                            in_=pMM[ob // 2][:, (ob % 2) * RT:
                                             (ob % 2 + 1) * RT],
                            func=AF.Relu))
                ev_up = s_act.n

                ld_dn0, pe_dn0 = s_ld.n, s_pe.n

                wdn_cnt = []

                @block.sync
                def _(sync, pe_dn0=pe_dn0):
                    for step in range(32):
                        if step < 2:
                            sync.wait_ge(s_pe.h, pe_dn0)
                        else:
                            sync.wait_ge(s_pe.h, pe_dn0 + step - 1)
                        wsrc = wdn_bf if step < 16 else wexp_bf
                        cc_ = step % 16
                        wdn_cnt.append(inc(
                            s_lp[step % 2], sync.dma_start(
                                out=wch[step % 2][:],
                                in_=wsrc[cc_ * 128:(cc_ + 1) * 128, :]), 16))

                act_dn0 = s_act.n

                @block.tensor
                def _(tensor, ld_dn0=ld_dn0, ev_up=ev_up, act_dn0=act_dn0):
                    tensor.wait_ge(s_act.h, ev_up)
                    for step in range(32):
                        tensor.wait_ge(s_lp[step % 2].h, wdn_cnt[step])
                        if step == 0:
                            for b in range(8):
                                pm.acquire(tensor, pMM[b])
                            ob_order = [x for b in range(8)
                                        for x in (2 * b + 1, 2 * b)]
                        else:
                            ob_order = list(range(16))
                        rhs = hid if step < 16 else xn2s_fm
                        cc_ = step % 16
                        for ob in ob_order:
                            ins_ = tensor.matmul(
                                pMM[ob // 2][:, (ob % 2) * RT:
                                             (ob % 2 + 1) * RT],
                                wch[step % 2][:, ob * 128:(ob + 1) * 128],
                                rhs[:, cc_ * RT:(cc_ + 1) * RT],
                                start=(step == 0 and ob % 2 == 1),
                                stop=(step == 31))
                        inc(s_pe, ins_)
                    for b in range(8):
                        pm.release(pMM[b], s_act.h, act_dn0 + 16)
                pe_dn_end = s_pe.n

                @block.scalar
                def _(scalar, pe_dn_end=pe_dn_end):
                    scalar.wait_ge(s_pe.h, pe_dn_end)
                    for db in range(16):
                        inc(s_act, scalar.copy(
                            out=fm[:, db * RT:(db + 1) * RT],
                            in_=pMM[db // 2][:, (db % 2) * RT:
                                             (db % 2 + 1) * RT]))
                ev_ffn = s_act.n

                for tt in range(2):
                    pe_f0 = s_pe.n

                    @block.tensor
                    def _(tensor, tt=tt, ev_ffn=ev_ffn):
                        tensor.wait_ge(s_act.h, ev_ffn)
                        for g in range(4):
                            bank = pMM[g]
                            pm.acquire(tensor, bank)
                            for m in range(4):
                                k = 4 * g + m
                                ins_ = tensor.transpose(
                                    bank[:, m * 128:(m + 1) * 128],
                                    fm[:, k * RT + tt * 128:
                                       k * RT + (tt + 1) * 128],
                                    ident_f[:])
                            inc(s_pe, ins_)
                            pm.release(bank, s_act.h, s_act.n + g + 1)

                    @block.scalar
                    def _(scalar, pe_f0=pe_f0):
                        for g in range(4):
                            scalar.wait_ge(s_pe.h, pe_f0 + g + 1)
                            inc(s_act, scalar.copy(
                                out=ao_tok.rearrange(
                                    "p (k f) -> p k f", k=16)[
                                    :, 4 * g:4 * g + 4, :],
                                in_=pMM[g].rearrange(
                                    "p (m f) -> p m f", m=4)))
                    ev_f = s_act.n

                    act_fl0, dve_fl0 = s_act.n, s_dve.n

                    @block.scalar
                    def _(scalar, act_fl0=act_fl0, dve_fl0=dve_fl0):
                        scalar.wait_ge(s_dve.h, dve_fl0)
                        inc(s_act, scalar.activation(
                            out=scr3[:], in_=ao_tok[:], func=AF.Square,
                            accum_out=sm[:, 1:2]))
                        inc(s_act, scalar.activation(
                            out=sm[:, 3:4], in_=sm[:, 1:2], func=AF.Copy,
                            scale=RINV, bias=EPS))
                        inc(s_act, scalar.activation(
                            out=scr3[:], in_=ao_tok[:], func=AF.Copy,
                            accum_out=sm[:, 0:1]))
                        inc(s_act, scalar.activation(
                            out=sm[:, 2:3], in_=sm[:, 0:1], func=AF.Square,
                            scale=RINV))
                        inc(s_act, scalar.activation(
                            out=sm[:, 4:5], in_=sm[:, 2:3], func=AF.Ln,
                            scale=-1.0, bias=sm[:, 3:4]))
                        inc(s_act, scalar.activation(
                            out=sm[:, 5:6], in_=sm[:, 4:5], func=AF.Exp,
                            scale=-0.5))
                        inc(s_act, scalar.activation(
                            out=sm[:, 6:7], in_=sm[:, 0:1], func=AF.Copy,
                            scale=RINV))

                    st_before = s_st.n

                    @block.vector
                    def _(vector, tt=tt, ev_f=ev_f, act_fl0=act_fl0,
                          st_before=st_before):
                        vector.wait_ge(s_act.h, ev_f)
                        if st_before > 0:
                            vector.wait_ge(s_st.h, st_before)
                        vector.wait_ge(s_act.h, act_fl0 + 7)
                        vector.tensor_scalar(
                            out=lnt[:], in0=ao_tok[:], scalar1=sm[:, 6:7],
                            scalar2=sm[:, 5:6], op0=ALU.subtract,
                            op1=ALU.mult)
                        inc(s_dve, vector.tensor_add(
                            out=out_t[:], in0=x2t[tt][:], in1=lnt[:]))
                    out_done = s_dve.n

                    @block.gpsimd
                    def _(gpsimd, tt=tt, g0=g0, out_done=out_done):
                        gpsimd.wait_ge(s_dve.h, out_done)
                        inc(s_st, gpsimd.dma_start(
                            out=out_tok[g0 + tt * 128:g0 + (tt + 1) * 128, :],
                            in_=out_t[:]), 16)

            @block.sync
            def _(sync):
                sync.wait_ge(s_st.h, s_st.n)

    return nc


_NC_CACHE = {}


def kernel(L0_attn_x, L0_attn_norm_w, L0_attn_W_skip, L0_attn_W_in,
           L0_attn_conv_w, L0_attn_conv_b, L0_attn_Wd1, L0_attn_Wd2,
           L0_attn_bd2, L0_attn_WB, L0_attn_WC, L0_attn_A_log, L0_attn_W_D,
           L0_attn_W_out, L0_ffn_W_up, L0_ffn_W_down, L0_ffn_W_gate,
           L0_ffn_W_expert):
    f = np.float32
    x = np.asarray(L0_attn_x, f).reshape(L, D)
    W_in = np.asarray(L0_attn_W_in, f)
    W_skip = np.asarray(L0_attn_W_skip, f)
    Wd1 = np.asarray(L0_attn_Wd1, f)
    Wd2 = np.asarray(L0_attn_Wd2, f)
    WB = np.asarray(L0_attn_WB, f)
    WC = np.asarray(L0_attn_WC, f)
    A_log = np.asarray(L0_attn_A_log, f)
    conv_w = np.asarray(L0_attn_conv_w, f)
    conv_b = np.asarray(L0_attn_conv_b, f)
    bd2 = np.asarray(L0_attn_bd2, f)
    W_D = np.asarray(L0_attn_W_D, f)
    W_out = np.asarray(L0_attn_W_out, f)
    norm_w = np.asarray(L0_attn_norm_w, f)
    W_up = np.asarray(L0_ffn_W_up, f)
    W_down = np.asarray(L0_ffn_W_down, f)
    W_gate = np.asarray(L0_ffn_W_gate, f)
    W_exp = np.asarray(L0_ffn_W_expert, f)

    ca = np.ascontiguousarray
    w_outT = ca(W_out.T)
    w_upT = ca(W_up.T)
    w_downT = ca(W_down.T)
    w_exp = ca(W_exp)
    w_gateT = ca(W_gate.T)
    ident = np.eye(128, dtype=f)
    nwr = ca(norm_w.reshape(16, 128).T)

    def col4(v):
        return ca(v.reshape(4, 128).T)

    in_maps = []
    for c in range(CORES):
        es = slice(ELOC * c, ELOC * (c + 1))
        in_maps.append(dict(
            x_tok=ca(x[TLOC * c:TLOC * (c + 1), :]),
            w_inT=ca(W_in[es].T),
            w_skipT=ca(W_skip[es].T),
            wd1T=ca(Wd1[:, es].T),
            wd2T=ca(Wd2[es].T),
            wbT=ca(WB[:, es].T),
            wcT=ca(WC[:, es].T),
            alog_r=ca(A_log[es].reshape(4, 128, N).transpose(1, 0, 2)
                      .reshape(128, 4 * N)),
            convw_r=ca(conv_w[es, 0, :].reshape(4, 128, DC)
                       .transpose(1, 0, 2).reshape(128, 4 * DC)),
            convb_r=col4(conv_b[es]),
            bd2_r=col4(bd2[es]),
            wd_r=col4(W_D[es]),
            norm_w=nwr,
            w_outT=w_outT,
            w_upT=w_upT,
            w_downT=w_downT,
            w_exp=w_exp,
            w_gateT=w_gateT,
            ident=ident,
        ))

    if "nc" not in _NC_CACHE:
        _NC_CACHE["nc"] = build_nc()
    nc = _NC_CACHE["nc"]
    res = run_bass_kernel_spmd(nc, in_maps, core_ids=list(range(CORES)))
    out = np.concatenate(
        [np.asarray(res.results[c]["out_tok"], f) for c in range(CORES)], 0)
    return out.reshape(1, L, D)


if __name__ == "__main__":
    import reference as _r
    ins = _r.setup_inputs()
    got = kernel(**{k: np.asarray(v) for k, v in ins.items()})
    want = np.asarray(_r.reference(**ins))
    err = np.abs(got - want)
    rel = float(np.mean((got - want) ** 2) / np.mean(want ** 2))
    print("maxabs", float(err.max()), "relvar", rel)


# version 11
# speedup vs baseline: 1.1468x; 1.1468x over previous
import contextlib
import numpy as np
import concourse.bass as bass
import concourse.mybir as mybir
from concourse.aot_env import allow_ici_all_to_all
from concourse.bass_utils import run_bass_kernel_spmd

F32 = mybir.dt.float32
BF16 = mybir.dt.bfloat16
AF = mybir.ActivationFunctionType
ALU = mybir.AluOpType
AX = mybir.AxisListType

CORES = 8
L = 16384
D = 2048
E = 4096
N = 16
DT = 128
DC = 4
ELOC = E // CORES
TLOC = L // CORES
CL = 1024
NCH = L // CL
EPS = 1e-5
RT = 256
NRD = TLOC // RT
RINV = 1.0 / D


class PsumMgr:
    def __init__(self):
        self.rel = {}

    def acquire(self, eng, bank):
        r = self.rel.get(id(bank))
        if r is not None:
            sem, cnt = r
            eng.wait_ge(sem, cnt)

    def release(self, bank, sem, cnt):
        self.rel[id(bank)] = (sem, cnt)


class Sem:
    def __init__(self, h):
        self.h = h
        self.n = 0


def build_nc():
    nc = bass.Bass("TRN2", target_bir_lowering=False)

    def din(name, shape, dtype=F32):
        return nc.dram_tensor(name, shape, dtype, kind="ExternalInput").ap()

    def dout(name, shape, dtype=F32):
        return nc.dram_tensor(name, shape, dtype, kind="ExternalOutput").ap()

    def dint(name, shape, dtype=F32):
        return nc.dram_tensor(name, shape, dtype).ap()

    x_tok = din("x_tok", [TLOC, D])
    w_inT = din("w_inT", [D, ELOC])
    w_skipT = din("w_skipT", [D, ELOC])
    wd1T = din("wd1T", [ELOC, DT])
    wd2T = din("wd2T", [DT, ELOC])
    wbT = din("wbT", [ELOC, N])
    wcT = din("wcT", [ELOC, N])
    alog_r = din("alog_r", [128, 4 * N])
    convw_r = din("convw_r", [128, 4 * DC])
    convb_r = din("convb_r", [128, 4])
    bd2_r = din("bd2_r", [128, 4])
    wd_r = din("wd_r", [128, 4])
    nwcol_in = din("norm_w", [128, 16])
    w_outT_f = din("w_outT", [E, D])
    w_upT_f = din("w_upT", [D, D])
    w_downT_f = din("w_downT", [D, D])
    w_exp_f = din("w_exp", [D, D])
    w_gateT_f = din("w_gateT", [D, 8])
    ident_in = din("ident", [128, 128])
    out_tok = dout("out_tok", [TLOC, D])

    ag_xn_in = dint("ag_xn_in", [1, D * TLOC], BF16)
    ag_xn_out = dint("ag_xn_out", [CORES, D * TLOC], BF16)
    ag_xn_v = ag_xn_out.rearrange("r (b p f) -> (r b) p f", p=128, f=512)
    ar_in = [dint(f"ar_in{i}", [DT + 2 * N, CL]) for i in range(2)]
    ar_out = [dint(f"ar_out{i}", [DT + 2 * N, CL]) for i in range(2)]
    bc_dram = [dint(f"bc_dram{i}", [2 * N, CL], BF16) for i in range(2)]
    y_a2a_in = dint("y_a2a_in", [CORES * 4 * 128, TLOC], BF16)
    y_a2a_out = dint("y_a2a_out", [CORES * 4 * 128, TLOC], BF16)
    wout_bf = dint("wout_bf", [E, D], BF16)
    wup_bf = dint("wup_bf", [D, D], BF16)
    wdn_bf = dint("wdn_bf", [D, D], BF16)
    wexp_bf = dint("wexp_bf", [D, D], BF16)

    with contextlib.ExitStack() as top:
        TE = top.enter_context
        ident_bf = TE(nc.sbuf_tensor("ident_bf", [128, 128], BF16))
        ident_f = TE(nc.sbuf_tensor("ident_f", [128, 128], F32))
        nwcol = TE(nc.sbuf_tensor("nwcol", [128, 16], F32))
        convb_sb = TE(nc.sbuf_tensor("convb_sb", [128, 4], F32))
        bd2_sb = TE(nc.sbuf_tensor("bd2_sb", [128, 4], F32))
        wdv_sb = TE(nc.sbuf_tensor("wdv_sb", [128, 4], F32))
        convw_sb = TE(nc.sbuf_tensor("convw_sb", [128, 16], F32))
        alog_sb = TE(nc.sbuf_tensor("alog_sb", [128, 64], F32))
        a_sb = TE(nc.sbuf_tensor("a_sb", [128, 64], F32))
        carry = TE(nc.sbuf_tensor("carry", [128, 64], F32))
        gate_sb = TE(nc.sbuf_tensor("gate_sb", [128, 128], BF16))
        sm = TE(nc.sbuf_tensor("sm", [128, 32], F32))
        cst = TE(nc.sbuf_tensor("cst", [128, 2], F32))

        s_ld = Sem(TE(nc.semaphore("s_ld")))
        s_gd = Sem(TE(nc.semaphore("s_gd")))
        s_cc = Sem(TE(nc.semaphore("s_cc")))
        s_pe = Sem(TE(nc.semaphore("s_pe")))
        s_act = Sem(TE(nc.semaphore("s_act")))
        s_dve = Sem(TE(nc.semaphore("s_dve")))
        s_a = Sem(TE(nc.semaphore("s_a")))
        s_scan = Sem(TE(nc.semaphore("s_scan")))
        s_bc = Sem(TE(nc.semaphore("s_bc")))
        s_ym = Sem(TE(nc.semaphore("s_ym")))
        s_st = Sem(TE(nc.semaphore("s_st")))
        s_wc = Sem(TE(nc.semaphore("s_wc")))
        s_lp = [Sem(TE(nc.semaphore("s_la"))), Sem(TE(nc.semaphore("s_lb")))]
        s_sp = Sem(TE(nc.semaphore("s_sp")))
        s_ag = Sem(TE(nc.semaphore("s_ag")))

        def inc(c, instr, k=1):
            instr.then_inc(c.h, k)
            c.n += k
            return c.n

        pm = PsumMgr()
        block = TE(nc.Block())

        @block.sync
        def _(sync):
            for src, dst in [
                (ident_in, ident_f), (convb_r, convb_sb), (bd2_r, bd2_sb),
                (wd_r, wdv_sb), (convw_r, convw_sb), (alog_r, alog_sb),
            ]:
                inc(s_ld, sync.dma_start(out=dst[:], in_=src[:]), 16)
            inc(s_ld, sync.dma_start(out=nwcol[:], in_=nwcol_in[:]), 16)
        ld_c0 = s_ld.n

        @block.scalar
        def _(scalar):
            scalar.wait_ge(s_ld.h, ld_c0)
            inc(s_act, scalar.activation(out=a_sb[:], in_=alog_sb[:],
                                         func=AF.Exp))
        act_c0 = s_act.n

        @block.vector
        def _(vector):
            vector.wait_ge(s_ld.h, ld_c0)
            vector.tensor_copy(out=ident_bf[:], in_=ident_f[:])
            vector.memset(cst[:, 0:1], EPS)
            vector.memset(cst[:, 1:2], 1.0)
            vector.wait_ge(s_act.h, act_c0)
            vector.tensor_scalar_mul(out=a_sb[:], in0=a_sb[:], scalar1=-1.0)
            inc(s_dve, vector.memset(carry[:], 0.0))

        with contextlib.ExitStack() as st12:
            SE = st12.enter_context
            w_inT_sb = SE(nc.sbuf_tensor("w_inT_sb", [128, 16 * ELOC], BF16))
            w_skT_sb = SE(nc.sbuf_tensor("w_skT_sb", [128, 16 * ELOC], BF16))
            wd1T_sb = SE(nc.sbuf_tensor("wd1T_sb", [128, 4 * DT], BF16))
            wd2T_sb = SE(nc.sbuf_tensor("wd2T_sb", [128, ELOC], BF16))
            wbT_sb = SE(nc.sbuf_tensor("wbT_sb", [128, 4 * N], BF16))
            wcT_sb = SE(nc.sbuf_tensor("wcT_sb", [128, 4 * N], BF16))

            with contextlib.ExitStack() as csc:
                CE = csc.enter_context
                wstg = [CE(nc.sbuf_tensor(f"wstg{i}", [128, D], F32))
                        for i in range(2)]
                wstg_b = [CE(nc.sbuf_tensor(f"wstg_b{i}", [128, D], BF16))
                          for i in range(2)]
                jobs = []
                for k in range(16):
                    jobs.append((w_inT[k * 128:(k + 1) * 128, :], ELOC, "sb",
                                 w_inT_sb, k * ELOC))
                for k in range(16):
                    jobs.append((w_skipT[k * 128:(k + 1) * 128, :], ELOC, "sb",
                                 w_skT_sb, k * ELOC))
                for i in range(4):
                    jobs.append((wd1T[i * 128:(i + 1) * 128, :], DT, "sb",
                                 wd1T_sb, i * DT))
                jobs.append((wd2T[:, :], ELOC, "sb", wd2T_sb, 0))
                for i in range(4):
                    jobs.append((wbT[i * 128:(i + 1) * 128, :], N, "sb",
                                 wbT_sb, i * N))
                for i in range(4):
                    jobs.append((wcT[i * 128:(i + 1) * 128, :], N, "sb",
                                 wcT_sb, i * N))
                for k in range(16):
                    jobs.append((w_gateT_f[k * 128:(k + 1) * 128, :], 8, "sb",
                                 gate_sb, k * 8))
                for blk in range(32):
                    jobs.append((w_outT_f[blk * 128:(blk + 1) * 128, :], D,
                                 "dr", wout_bf, blk * 128))
                for wf, wb_ in [(w_upT_f, wup_bf), (w_downT_f, wdn_bf),
                                (w_exp_f, wexp_bf)]:
                    for blk in range(16):
                        jobs.append((wf[blk * 128:(blk + 1) * 128, :], D,
                                     "dr", wb_, blk * 128))

                dr_of = {}
                ndr = 0
                for idx, j in enumerate(jobs):
                    if j[2] == "dr":
                        dr_of[idx] = ndr
                        ndr += 1

                with contextlib.ExitStack() as s1s:
                    S1 = s1s.enter_context
                    xs1 = [S1(nc.sbuf_tensor(f"xs1_{i}", [128, D], F32))
                           for i in range(2)]
                    scr1 = S1(nc.sbuf_tensor("scr1", [128, D], F32))
                    ln1 = S1(nc.sbuf_tensor("ln1", [128, D], F32))
                    xn_bf = S1(nc.sbuf_tensor("xn_bf", [128, D], BF16))
                    xnT_g = S1(nc.sbuf_tensor("xnT_g", [128, 16 * 512], BF16))
                    ptp = [S1(nc.psum_tensor(f"ptp{g}", [128, 512], BF16))
                           for g in range(4)]

                    ld_b, act_b, dve_b = s_ld.n, s_act.n, s_dve.n
                    pe_b, gd_b = s_pe.n, s_gd.n

                    def act1(tt, k):
                        return act_b + 14 * tt + k

                    def dve1(tt, k):
                        return dve_b + 2 * tt + k

                    x_ld_cnt = []

                    @block.sync
                    def _(sync):
                        for tt in range(16):
                            sync.wait_ge(s_dve.h,
                                         dve1(tt - 2, 1) if tt >= 2 else 0)
                            x_ld_cnt.append(inc(
                                s_lp[tt % 2], sync.dma_start(
                                    out=xs1[tt % 2][:],
                                    in_=x_tok[tt * 128:(tt + 1) * 128, :]),
                                16))

                    @block.tensor
                    def _(tensor):
                        for tt in range(16):
                            tensor.wait_ge(s_dve.h, dve1(tt, 2))
                            if tt >= 1:
                                tensor.wait_ge(s_act.h, act1(tt - 1, 14))
                            for g in range(4):
                                for m in range(4):
                                    k = 4 * g + m
                                    ins_ = tensor.transpose(
                                        ptp[g][:, m * 128:(m + 1) * 128],
                                        xn_bf[:, k * 128:(k + 1) * 128],
                                        ident_bf[:])
                                inc(s_pe, ins_)

                    @block.scalar
                    def _(scalar):
                        scalar.wait_ge(s_dve.h, dve_b)
                        for tt in range(16):
                            if tt >= 1:
                                scalar.wait_ge(s_dve.h, dve1(tt - 1, 2))
                            scalar.wait_ge(s_lp[tt % 2].h, x_ld_cnt[tt])
                            inc(s_act, scalar.activation(
                                out=scr1[:], in_=xs1[tt % 2][:],
                                func=AF.Square, accum_out=sm[:, 1:2]))
                            inc(s_act, scalar.activation(
                                out=sm[:, 3:4], in_=sm[:, 1:2],
                                func=AF.Copy, scale=RINV, bias=EPS))
                            inc(s_act, scalar.activation(
                                out=scr1[:], in_=xs1[tt % 2][:],
                                func=AF.Copy, accum_out=sm[:, 0:1]))
                            inc(s_act, scalar.activation(
                                out=sm[:, 2:3], in_=sm[:, 0:1],
                                func=AF.Square, scale=RINV))
                            inc(s_act, scalar.activation(
                                out=sm[:, 4:5], in_=sm[:, 2:3],
                                func=AF.Ln, scale=-1.0, bias=sm[:, 3:4]))
                            inc(s_act, scalar.activation(
                                out=sm[:, 5:6], in_=sm[:, 4:5],
                                func=AF.Exp, scale=-0.5))
                            inc(s_act, scalar.activation(
                                out=sm[:, 6:7], in_=sm[:, 0:1],
                                func=AF.Copy, scale=RINV))
                            scalar.wait_ge(s_dve.h, dve1(tt, 1))
                            inc(s_act, scalar.activation(
                                out=scr1[:], in_=ln1[:], func=AF.Square,
                                accum_out=sm[:, 7:8]))
                            inc(s_act, scalar.activation(
                                out=sm[:, 8:9], in_=sm[:, 7:8],
                                func=AF.Ln, scale=RINV, bias=cst[:, 0:1]))
                            inc(s_act, scalar.activation(
                                out=sm[:, 9:10], in_=sm[:, 8:9],
                                func=AF.Exp, scale=-0.5))
                            for g in range(4):
                                scalar.wait_ge(s_pe.h, pe_b + 4 * tt + g + 1)
                                if tt >= 4:
                                    scalar.wait_ge(s_gd.h,
                                                   gd_b + 256 * (tt // 4))
                                dst = xnT_g.rearrange(
                                    "p (k f) -> p k f", k=16)[
                                    :, 4 * g:4 * g + 4,
                                    (tt % 4) * 128:(tt % 4 + 1) * 128]
                                inc(s_act, scalar.copy(
                                    out=dst,
                                    in_=ptp[g].rearrange(
                                        "p (m f) -> p m f", m=4)))

                    @block.vector
                    def _(vector):
                        for tt in range(16):
                            vector.wait_ge(s_act.h, act1(tt, 7))
                            inc(s_dve, vector.tensor_scalar(
                                out=ln1[:], in0=xs1[tt % 2][:],
                                scalar1=sm[:, 6:7], scalar2=sm[:, 5:6],
                                op0=ALU.subtract, op1=ALU.mult))
                            vector.wait_ge(s_act.h, act1(tt, 10))
                            vector.wait_ge(s_pe.h, pe_b + 4 * tt)
                            inc(s_dve, vector.tensor_scalar_mul(
                                out=xn_bf[:], in0=ln1[:],
                                scalar1=sm[:, 9:10]))

                    @block.gpsimd
                    def _(gpsimd):
                        for grp in range(4):
                            gpsimd.wait_ge(s_act.h, act1(4 * grp + 3, 14))
                            for k in range(16):
                                gk = grp * 16 + k
                                dst = ag_xn_in.rearrange(
                                    "r (b p f) -> (r b) p f", p=128, f=512)[gk]
                                inc(s_gd, gpsimd.dma_start(
                                    out=dst,
                                    in_=xnT_g[:, k * 512:(k + 1) * 512]), 16)
                        gpsimd.wait_ge(s_gd.h, s_gd.n)
                        inc(s_cc, gpsimd.collective_compute(
                            "AllGather", ALU.bypass,
                            replica_groups=[list(range(CORES))],
                            ins=[ag_xn_in.opt()], outs=[ag_xn_out.opt()]))
                    cc_agxn = s_cc.n

                ld_w0, act_w0, wc_w0, gd_w0 = s_ld.n, s_act.n, s_wc.n, s_gd.n
                cons_sem = []
                na, nw = act_w0, wc_w0
                for idx, j in enumerate(jobs):
                    if j[2] == "sb" and idx < 32:
                        na += 1
                        cons_sem.append((s_act, na))
                    else:
                        nw += 1
                        cons_sem.append((s_wc, nw))
                n_wc_sb = sum(1 for idx, j in enumerate(jobs)
                              if j[2] == "sb" and idx >= 32)

                w_ld_cnt = []

                @block.sync
                def _(sync):
                    for idx, (src, cols, kind, buf, off) in enumerate(jobs):
                        if idx >= 2:
                            sem, cnt = cons_sem[idx - 2]
                            sync.wait_ge(sem.h, cnt)
                        w_ld_cnt.append(inc(
                            s_lp[idx % 2], sync.dma_start(
                                out=wstg[idx % 2][:, 0:cols], in_=src), 16))

                @block.scalar
                def _(scalar):
                    for idx, (src, cols, kind, buf, off) in enumerate(jobs):
                        if not (kind == "sb" and idx < 32):
                            continue
                        scalar.wait_ge(s_lp[idx % 2].h, w_ld_cnt[idx])
                        inc(s_act, scalar.mul(
                            out=buf[:, off:off + cols],
                            in_=wstg[idx % 2][:, 0:cols],
                            mul=nwcol[:, (idx % 16):(idx % 16) + 1]))

                act_wsb_done = s_act.n

                @block.gpsimd
                def _(gpsimd):
                    for idx, (src, cols, kind, buf, off) in enumerate(jobs):
                        if kind == "sb" and idx < 32:
                            continue
                        gpsimd.wait_ge(s_lp[idx % 2].h, w_ld_cnt[idx])
                        if kind == "sb":
                            inc(s_wc, gpsimd.tensor_copy(
                                out=buf[:, off:off + cols],
                                in_=wstg[idx % 2][:, 0:cols]))
                        else:
                            d = dr_of[idx]
                            if d >= 2:
                                gpsimd.wait_ge(s_gd.h, gd_w0 + 16 * (d - 1))
                            inc(s_wc, gpsimd.tensor_copy(
                                out=wstg_b[d % 2][:, 0:cols],
                                in_=wstg[idx % 2][:, 0:cols]))
                            inc(s_gd, gpsimd.dma_start(
                                out=buf[off:off + 128, :],
                                in_=wstg_b[d % 2][:, 0:cols]), 16)
                wc_sb_done = wc_w0 + n_wc_sb
                wc_all_done = s_wc.n
                gd_dr_done = s_gd.n

            with contextlib.ExitStack() as s2s:
                S2 = s2s.enter_context
                xn_rhs = S2(nc.sbuf_tensor("xn_rhs", [128, 16 * 512], BF16))
                u_proj = [S2(nc.sbuf_tensor(f"u_proj{i}", [128, CL + 4], F32))
                          for i in range(4)]
                ss_sb = [S2(nc.sbuf_tensor(f"ss{i}", [128, CL], BF16))
                         for i in range(4)]
                u_bf = [S2(nc.sbuf_tensor(f"u_bf{i}", [128, CL], BF16))
                        for i in range(4)]
                delta = [S2(nc.sbuf_tensor(f"delta{i}", [128, CL], F32))
                         for i in range(4)]
                du_bf = [S2(nc.sbuf_tensor(f"du_bf{i}", [128, CL], BF16))
                         for i in range(4)]
                d1_sb = S2(nc.sbuf_tensor("d1_sb", [128, CL], F32))
                d1_bf = S2(nc.sbuf_tensor("d1_bf", [128, CL], BF16))
                bp_sb = S2(nc.sbuf_tensor("bp_sb", [16, CL], F32))
                cp_sb = S2(nc.sbuf_tensor("cp_sb", [16, CL], F32))
                bcb_sb = S2(nc.sbuf_tensor("bcb_sb", [16, CL], BF16))
                ccb_sb = S2(nc.sbuf_tensor("ccb_sb", [16, CL], BF16))
                uc = [S2(nc.sbuf_tensor(f"uc{i}", [128, CL], F32))
                      for i in range(2)]
                zbf = [S2(nc.sbuf_tensor(f"zbf{i}", [128, 512], F32))
                       for i in range(2)]
                zcf = [S2(nc.sbuf_tensor(f"zcf{i}", [128, 512], F32))
                       for i in range(2)]
                B_bc = [S2(nc.sbuf_tensor(f"B_bc{i}", [128, CL], BF16))
                        for i in range(3)]
                C_bc = [S2(nc.sbuf_tensor(f"C_bc{i}", [128, CL], BF16))
                        for i in range(3)]
                a_bf = [S2(nc.sbuf_tensor(f"a_bf{i}", [128, CL], BF16))
                        for i in range(2)]
                b_bf = S2(nc.sbuf_tensor("b_bf", [128, CL], BF16))
                h_d = [S2(nc.sbuf_tensor(f"h_d{i}", [128, CL], BF16))
                       for i in range(2)]
                t_f = S2(nc.sbuf_tensor("t_f", [128, CL], F32))
                t_p = S2(nc.sbuf_tensor("t_p", [128, CL], F32))
                y_acc = [S2(nc.sbuf_tensor(f"y_acc{i}", [128, CL], F32))
                         for i in range(4)]
                ym_bf = [S2(nc.sbuf_tensor(f"ym_bf{i}", [128, CL], BF16))
                         for i in range(2)]
                pU = [S2(nc.psum_tensor(f"pU{i}", [128, 512], F32))
                      for i in range(2)]
                pS = [S2(nc.psum_tensor(f"pS{i}", [128, 512], F32))
                      for i in range(2)]
                pD1 = S2(nc.psum_tensor("pD1", [128, 512], F32))
                pB = S2(nc.psum_tensor("pB", [16, 512], F32))
                pC = S2(nc.psum_tensor("pC", [16, 512], F32))
                pD2 = S2(nc.psum_tensor("pD2", [128, 512], F32))

                for ch in range(NCH):
                    jj, half = ch // 2, ch % 2

                    @block.vector
                    def _(vector, ch=ch):
                        if ch == 0:
                            vector.wait_ge(s_wc.h, wc_all_done)
                            vector.wait_ge(s_gd.h, gd_dr_done)
                        for i in range(4):
                            if ch == 0:
                                ins_ = vector.memset(u_proj[i][:, 0:3], 0.0)
                            else:
                                ins_ = vector.tensor_copy(
                                    out=u_proj[i][:, 0:3],
                                    in_=u_proj[i][:, CL:CL + 3])
                        inc(s_dve, ins_)
                    halo_cnt = s_dve.n

                    ld_ch, pe_ch = s_ld.n, s_pe.n

                    @block.sync
                    def _(sync, ch=ch, ld_ch=ld_ch, pe_ch=pe_ch):
                        if ch == 0:
                            sync.wait_ge(s_cc.h, cc_agxn)
                            sync.wait_ge(s_wc.h, wc_all_done)
                            sync.wait_ge(s_gd.h, gd_dr_done)
                        else:
                            sync.wait_ge(s_ym.h, 4 * ch)
                        for sub in range(2):
                            if sub == 1:
                                sync.wait_ge(s_pe.h, pe_ch + 8)
                            g0 = ch * 2 + sub
                            oc, gg = g0 // 4, g0 % 4
                            for k in range(16):
                                inc(s_ld, sync.dma_start(
                                    out=xn_rhs[:, k * 512:(k + 1) * 512],
                                    in_=ag_xn_v[oc * 64 + gg * 16 + k]), 16)

                    act_ev0 = s_act.n
                    grp_list = []
                    for sub in range(2):
                        for i in range(4):
                            grp_list.append(("u", i, sub))
                        for i in range(4):
                            grp_list.append(("sk", i, sub))

                    @block.tensor
                    def _(tensor, ch=ch, ld_ch=ld_ch, act_ev0=act_ev0):
                        if ch == 0:
                            tensor.wait_ge(s_act.h, act_wsb_done)
                            tensor.wait_ge(s_wc.h, wc_sb_done)
                        for g, (kind, i, sub) in enumerate(grp_list):
                            tensor.wait_ge(s_ld.h, ld_ch + 256 * (sub + 1))
                            bank = (pU if kind == "u" else pS)[i % 2]
                            pm.acquire(tensor, bank)
                            wsb = w_inT_sb if kind == "u" else w_skT_sb
                            for k in range(16):
                                ins_ = tensor.matmul(
                                    bank[:],
                                    wsb[:, k * ELOC + i * 128:
                                        k * ELOC + (i + 1) * 128],
                                    xn_rhs[:, k * 512:(k + 1) * 512],
                                    start=(k == 0), stop=(k == 15))
                            inc(s_pe, ins_)
                            pm.release(bank, s_act.h, act_ev0 + g + 1)

                    @block.scalar
                    def _(scalar, pe_ch=pe_ch, halo_cnt=halo_cnt):
                        scalar.wait_ge(s_dve.h, halo_cnt)
                        for g, (kind, i, sub) in enumerate(grp_list):
                            scalar.wait_ge(s_pe.h, pe_ch + g + 1)
                            bank = (pU if kind == "u" else pS)[i % 2]
                            if kind == "u":
                                inc(s_act, scalar.copy(
                                    out=u_proj[i][:, 3 + sub * 512:
                                                  3 + (sub + 1) * 512],
                                    in_=bank[:]))
                            else:
                                inc(s_act, scalar.activation(
                                    out=ss_sb[i][:, sub * 512:(sub + 1) * 512],
                                    in_=bank[:], func=AF.Silu))

                    conv0 = s_dve.n
                    silu_b = s_act.n

                    @block.vector
                    def _(vector, act_ev0=act_ev0, silu_b=silu_b):
                        for i in range(4):
                            vector.wait_ge(s_act.h, act_ev0 + 8 + i + 1)
                            if i >= 1:
                                vector.wait_ge(s_act.h, silu_b + i)
                            vector.tensor_scalar_mul(
                                out=uc[0][:], in0=u_proj[i][:, 0:CL],
                                scalar1=convw_sb[:, 4 * i:4 * i + 1])
                            vector.scalar_tensor_tensor(
                                out=uc[1][:], in0=u_proj[i][:, 1:1 + CL],
                                scalar=convw_sb[:, 4 * i + 1:4 * i + 2],
                                in1=uc[0][:], op0=ALU.mult, op1=ALU.add)
                            vector.scalar_tensor_tensor(
                                out=uc[0][:], in0=u_proj[i][:, 2:2 + CL],
                                scalar=convw_sb[:, 4 * i + 2:4 * i + 3],
                                in1=uc[1][:], op0=ALU.mult, op1=ALU.add)
                            inc(s_dve, vector.scalar_tensor_tensor(
                                out=uc[1][:], in0=u_proj[i][:, 3:3 + CL],
                                scalar=convw_sb[:, 4 * i + 3:4 * i + 4],
                                in1=uc[0][:], op0=ALU.mult, op1=ALU.add))

                    @block.scalar
                    def _(scalar, conv0=conv0):
                        for i in range(4):
                            scalar.wait_ge(s_dve.h, conv0 + i + 1)
                            inc(s_act, scalar.activation(
                                out=u_bf[i][:], in_=uc[1][:], func=AF.Silu,
                                bias=convb_sb[:, i:i + 1]))
                    silu_end = s_act.n

                    d1bc = [("d1", 0), ("B", 0), ("C", 0),
                            ("d1", 1), ("B", 1), ("C", 1)]
                    pe_d1b = s_pe.n
                    act_ev1 = s_act.n

                    @block.tensor
                    def _(tensor, ch=ch, silu_end=silu_end, act_ev1=act_ev1):
                        if ch == 0:
                            tensor.wait_ge(s_wc.h, wc_sb_done)
                        tensor.wait_ge(s_act.h, silu_end)
                        for g, (kind, sub) in enumerate(d1bc):
                            bank = {"d1": pD1, "B": pB, "C": pC}[kind]
                            wsb = {"d1": wd1T_sb, "B": wbT_sb,
                                   "C": wcT_sb}[kind]
                            w_n = {"d1": DT, "B": N, "C": N}[kind]
                            pm.acquire(tensor, bank)
                            for i in range(4):
                                ins_ = tensor.matmul(
                                    bank[:],
                                    wsb[:, i * w_n:(i + 1) * w_n],
                                    u_bf[i][:, sub * 512:(sub + 1) * 512],
                                    start=(i == 0), stop=(i == 3))
                            inc(s_pe, ins_)
                            pm.release(bank, s_act.h, act_ev1 + g + 1)

                    @block.scalar
                    def _(scalar, pe_d1b=pe_d1b):
                        for g, (kind, sub) in enumerate(d1bc):
                            scalar.wait_ge(s_pe.h, pe_d1b + g + 1)
                            c0, c1 = sub * 512, (sub + 1) * 512
                            if kind == "d1":
                                inc(s_act, scalar.copy(out=d1_sb[:, c0:c1],
                                                       in_=pD1[:]))
                            elif kind == "B":
                                inc(s_act, scalar.copy(out=bp_sb[:, c0:c1],
                                                       in_=pB[:]))
                            else:
                                inc(s_act, scalar.copy(out=cp_sb[:, c0:c1],
                                                       in_=pC[:]))
                    evd1_end = s_act.n

                    @block.gpsimd
                    def _(gpsimd, ch=ch, evd1_end=evd1_end):
                        gpsimd.wait_ge(s_act.h, evd1_end)
                        gpsimd.wait_ge(s_gd.h, s_gd.n)
                        if ch >= 2:
                            gpsimd.wait_ge(s_cc.h, s_cc.n)
                        inc(s_gd, gpsimd.dma_start(
                            out=ar_in[ch % 2][0:DT, :], in_=d1_sb[:]), 16)
                        inc(s_gd, gpsimd.dma_start(
                            out=ar_in[ch % 2][DT:DT + N, :], in_=bp_sb[:]), 16)
                        inc(s_gd, gpsimd.dma_start(
                            out=ar_in[ch % 2][DT + N:DT + 2 * N, :],
                            in_=cp_sb[:]), 16)
                        gpsimd.wait_ge(s_gd.h, s_gd.n)
                        inc(s_cc, gpsimd.collective_compute(
                            "AllReduce", ALU.add,
                            replica_groups=[list(range(CORES))],
                            ins=[ar_in[ch % 2].opt()],
                            outs=[ar_out[ch % 2].opt()]))
                        gpsimd.wait_ge(s_cc.h, s_cc.n)
                        inc(s_gd, gpsimd.dma_start(
                            out=d1_sb[:], in_=ar_out[ch % 2][0:DT, :]), 16)
                        inc(s_gd, gpsimd.dma_start(
                            out=bp_sb[:],
                            in_=ar_out[ch % 2][DT:DT + N, :]), 16)
                        inc(s_gd, gpsimd.dma_start(
                            out=cp_sb[:],
                            in_=ar_out[ch % 2][DT + N:DT + 2 * N, :]), 16)
                    gd_arl = s_gd.n

                    @block.vector
                    def _(vector, gd_arl=gd_arl):
                        vector.wait_ge(s_gd.h, gd_arl)
                        vector.tensor_copy(out=d1_bf[:], in_=d1_sb[:])
                        vector.tensor_copy(out=bcb_sb[:], in_=bp_sb[:])
                        inc(s_dve, vector.tensor_copy(out=ccb_sb[:],
                                                      in_=cp_sb[:]))
                    cast_cnt = s_dve.n

                    @block.gpsimd
                    def _(gpsimd, ch=ch, cast_cnt=cast_cnt):
                        gpsimd.wait_ge(s_dve.h, cast_cnt)
                        inc(s_gd, gpsimd.dma_start(
                            out=bc_dram[ch % 2][0:N, :], in_=bcb_sb[:]), 16)
                        inc(s_gd, gpsimd.dma_start(
                            out=bc_dram[ch % 2][N:2 * N, :], in_=ccb_sb[:]),
                            16)
                    gd_bcd = s_gd.n

                    dve_sp0 = s_dve.n
                    act_sp0 = s_act.n

                    def zc_cnt(g, base=dve_sp0):
                        return base + (1 if g == 0 else 2 * g)

                    def mx_cnt(g, base=dve_sp0):
                        return base + (2 * g + 3 if g <= 6 else 16)

                    def du_cnt(i, base=dve_sp0):
                        return base + 16 + i + 1

                    d2banks = [pU[0], pU[1], pS[0], pS[1], pD1, pD2,
                               pU[0], pU[1]]
                    pe_d2 = s_pe.n

                    @block.tensor
                    def _(tensor, ch=ch, cast_cnt=cast_cnt, dve_sp0=dve_sp0):
                        if ch == 0:
                            tensor.wait_ge(s_wc.h, wc_sb_done)
                        tensor.wait_ge(s_dve.h, cast_cnt)
                        for g in range(8):
                            i, sub = g // 2, g % 2
                            bank = d2banks[g]
                            pm.acquire(tensor, bank)
                            ins_ = tensor.matmul(
                                bank[:],
                                wd2T_sb[:, i * 128:(i + 1) * 128],
                                d1_bf[:, sub * 512:(sub + 1) * 512],
                                start=True, stop=True)
                            inc(s_pe, ins_)
                            pm.release(bank, s_dve.h, zc_cnt(g))

                    @block.vector
                    def _(vector, pe_d2=pe_d2, act_sp0=act_sp0,
                          dve_sp0=dve_sp0):
                        for g in range(8):
                            i = g // 2
                            bank = d2banks[g]
                            vector.wait_ge(s_pe.h, pe_d2 + g + 1)
                            if g >= 2:
                                vector.wait_ge(s_act.h, act_sp0 + g - 1)
                            vector.tensor_scalar(
                                out=zbf[g % 2][:], in0=bank[:],
                                scalar1=bd2_sb[:, i:i + 1], scalar2=None,
                                op0=ALU.add)
                            inc(s_dve, vector.tensor_scalar_min(
                                out=zcf[g % 2][:], in0=zbf[g % 2][:],
                                scalar1=80.0))
                            if g >= 1:
                                gp = g - 1
                                ip, sp_ = gp // 2, gp % 2
                                vector.wait_ge(s_act.h, act_sp0 + gp + 1)
                                inc(s_dve, vector.tensor_max(
                                    out=delta[ip][:,
                                                  sp_ * 512:(sp_ + 1) * 512],
                                    in0=zbf[gp % 2][:], in1=zcf[gp % 2][:]))
                        vector.wait_ge(s_act.h, act_sp0 + 8)
                        inc(s_dve, vector.tensor_max(
                            out=delta[3][:, 512:1024],
                            in0=zbf[1][:], in1=zcf[1][:]))
                        for i in range(4):
                            vector.tensor_copy(out=b_bf[:], in_=delta[i][:])
                            inc(s_dve, vector.tensor_mul(
                                out=du_bf[i][:], in0=b_bf[:], in1=u_bf[i][:]))

                    @block.scalar
                    def _(scalar, dve_sp0=dve_sp0):
                        for g in range(8):
                            scalar.wait_ge(s_dve.h, zc_cnt(g, dve_sp0))
                            scalar.activation(out=uc[g % 2][:, 0:512],
                                              in_=zcf[g % 2][:], func=AF.Exp)
                            inc(s_act, scalar.activation(
                                out=zcf[g % 2][:], in_=uc[g % 2][:, 0:512],
                                func=AF.Ln, bias=cst[:, 1:2]))

                    a_b = s_a.n
                    scan_b = s_scan.n

                    @block.scalar
                    def _(scalar, a_b=a_b, scan_b=scan_b, dve_sp0=dve_sp0):
                        for idx in range(64):
                            n_, i = idx // 4, idx % 4
                            scalar.wait_ge(s_dve.h, du_cnt(i, dve_sp0))
                            if idx >= 2:
                                scalar.wait_ge(s_scan.h, scan_b + idx - 1)
                            inc(s_a, scalar.activation(
                                out=a_bf[idx % 2][:], in_=delta[i][:],
                                func=AF.Exp,
                                scale=a_sb[:, i * N + n_:i * N + n_ + 1]))

                    sp_b = s_sp.n
                    bc_b = s_bc.n

                    @block.vector
                    def _(vector, a_b=a_b, bc_b=bc_b, sp_b=sp_b,
                          scan_b=scan_b):
                        def carry_cp(j):
                            ii, nn = j % 4, j // 4
                            vector.tensor_copy(
                                out=carry[:, ii * N + nn:ii * N + nn + 1],
                                in_=h_d[j % 2][:, CL - 1:CL])

                        for idx in range(64):
                            n_, i = idx // 4, idx % 4
                            vector.wait_ge(s_a.h, a_b + idx + 1)
                            if i == 0:
                                vector.wait_ge(s_bc.h, bc_b + 32 * (n_ + 1))
                            vector.tensor_mul(out=b_bf[:], in0=du_bf[i][:],
                                              in1=B_bc[n_ % 3][:])
                            vector.tensor_tensor_scan(
                                out=h_d[idx % 2][:], data0=a_bf[idx % 2][:],
                                data1=b_bf[:],
                                initial=carry[:, i * N + n_:i * N + n_ + 1],
                                op0=ALU.mult, op1=ALU.add)
                            vector.tensor_mul(out=t_f[:],
                                              in0=h_d[idx % 2][:],
                                              in1=C_bc[n_ % 3][:])
                            if n_ == 0:
                                ins_ = vector.tensor_copy(out=y_acc[i][:],
                                                          in_=t_f[:])
                            else:
                                ins_ = vector.tensor_add(
                                    out=y_acc[i][:], in0=y_acc[i][:],
                                    in1=t_f[:])
                            inc(s_scan, ins_)
                            if idx >= 1:
                                carry_cp(idx - 1)
                        vector.memset(b_bf[:, 0:8], 0.0)
                        inc(s_scan, vector.memset(b_bf[:, 8:16], 0.0))
                        carry_cp(63)

                    @block.gpsimd
                    def _(gpsimd, ch=ch, gd_bcd=gd_bcd, scan_b=scan_b,
                          bc_b=bc_b):
                        def bcast(n_):
                            inc(s_bc, gpsimd.dma_start(
                                out=B_bc[n_ % 3][:],
                                in_=bc_dram[ch % 2][n_:n_ + 1, :]
                                .broadcast_to((128, CL))), 16)
                            inc(s_bc, gpsimd.dma_start(
                                out=C_bc[n_ % 3][:],
                                in_=bc_dram[ch % 2][N + n_:N + n_ + 1, :]
                                .broadcast_to((128, CL))), 16)

                        gpsimd.wait_ge(s_gd.h, gd_bcd)
                        gpsimd.wait_ge(s_scan.h, scan_b)
                        for n_ in range(16):
                            if n_ >= 3:
                                gpsimd.wait_ge(s_scan.h,
                                               scan_b + 4 * (n_ - 2))
                            bcast(n_)

                    gd_ym0 = s_gd.n

                    @block.vector
                    def _(vector, gd_ym0=gd_ym0):
                        vector.wait_ge(s_gd.h, gd_ym0)
                        for i in range(4):
                            if i >= 2:
                                vector.wait_ge(s_gd.h, gd_ym0 + 16 * (i - 1))
                            vector.scalar_tensor_tensor(
                                out=t_f[:], in0=u_bf[i][:],
                                scalar=wdv_sb[:, i:i + 1], in1=y_acc[i][:],
                                op0=ALU.mult, op1=ALU.add)
                            vector.tensor_copy(out=b_bf[:], in_=t_f[:])
                            inc(s_ym, vector.tensor_mul(
                                out=ym_bf[i % 2][:], in0=b_bf[:],
                                in1=ss_sb[i][:]))

                    ym_base = s_ym.n - 4

                    @block.gpsimd
                    def _(gpsimd, jj=jj, half=half, ym_base=ym_base):
                        for i in range(4):
                            gpsimd.wait_ge(s_ym.h, ym_base + i + 1)
                            r0 = (jj * 4 + i) * 128
                            inc(s_gd, gpsimd.dma_start(
                                out=y_a2a_in[r0:r0 + 128,
                                             half * CL:(half + 1) * CL],
                                in_=ym_bf[i % 2][:]), 16)

                @block.gpsimd
                def _(gpsimd):
                    gpsimd.wait_ge(s_gd.h, s_gd.n)
                    with allow_ici_all_to_all():
                        inc(s_cc, gpsimd.collective_compute(
                            "AllToAll", ALU.bypass,
                            replica_groups=[list(range(CORES))],
                            ins=[y_a2a_in.opt()], outs=[y_a2a_out.opt()]))
                cc_y = s_cc.n

        bar_act, bar_dve = s_act.n, s_dve.n

        with contextlib.ExitStack() as s3s:
            S3 = s3s.enter_context
            y_sb = S3(nc.sbuf_tensor("y_sb", [128, 32 * RT], BF16))
            wch = [S3(nc.sbuf_tensor(f"wch{i}", [128, D], BF16))
                   for i in range(2)]
            fm = S3(nc.sbuf_tensor("fm", [128, 16 * RT], F32))
            hid = S3(nc.sbuf_tensor("hid", [128, 16 * RT], BF16))
            xn2_fm = S3(nc.sbuf_tensor("xn2_fm", [128, 16 * RT], BF16))
            xn2s_fm = S3(nc.sbuf_tensor("xn2s_fm", [128, 16 * RT], BF16))
            xs3 = [S3(nc.sbuf_tensor(f"xs3_{i}", [128, D], F32))
                   for i in range(2)]
            x2t = [S3(nc.sbuf_tensor(f"x2t{i}", [128, D], F32))
                   for i in range(2)]
            xn2t = [S3(nc.sbuf_tensor(f"xn2t{i}", [128, D], BF16))
                    for i in range(2)]
            ao_tok = S3(nc.sbuf_tensor("ao_tok", [128, D], F32))
            scr3 = S3(nc.sbuf_tensor("scr3", [128, D], F32))
            lnt = S3(nc.sbuf_tensor("lnt", [128, D], F32))
            out_t = S3(nc.sbuf_tensor("out_t", [128, D], F32))
            glog = S3(nc.sbuf_tensor("glog", [8, RT], F32))
            ltok = S3(nc.sbuf_tensor("ltok", [128, 16], F32))
            jnk3 = S3(nc.sbuf_tensor("jnk3", [128, 8], F32))
            sgate = [S3(nc.sbuf_tensor(f"sgate{i}", [128, 1], F32))
                     for i in range(2)]
            pMM = [S3(nc.psum_tensor(f"pMM{i}", [128, 512], F32))
                   for i in range(8)]

            for rnd in range(NRD):
                g0 = rnd * RT
                ld_r0, pe_r0, dve_r0 = s_ld.n, s_pe.n, s_dve.n

                wch_cnt = []

                @block.sync
                def _(sync, rnd=rnd, g0=g0, ld_r0=ld_r0, pe_r0=pe_r0,
                      dve_r0=dve_r0):
                    if rnd == 0:
                        sync.wait_ge(s_cc.h, cc_y)
                        sync.wait_ge(s_dve.h, bar_dve)
                        sync.wait_ge(s_act.h, bar_act)
                    else:
                        sync.wait_ge(s_pe.h, pe_r0)
                        sync.wait_ge(s_dve.h, dve_r0)
                    for et in range(32):
                        inc(s_ld, sync.dma_start(
                            out=y_sb[:, et * RT:(et + 1) * RT],
                            in_=y_a2a_out[et * 128:(et + 1) * 128,
                                          g0:g0 + RT]), 16)
                    for tt in range(2):
                        inc(s_ld, sync.dma_start(
                            out=xs3[tt][:],
                            in_=x_tok[g0 + tt * 128:g0 + (tt + 1) * 128, :]),
                            16)
                    for ec in range(32):
                        if ec >= 2:
                            sync.wait_ge(s_pe.h, pe_r0 + ec - 1)
                        wch_cnt.append(inc(
                            s_lp[ec % 2], sync.dma_start(
                                out=wch[ec % 2][:],
                                in_=wout_bf[ec * 128:(ec + 1) * 128, :]),
                            16))

                act_r0 = s_act.n

                @block.tensor
                def _(tensor, ld_r0=ld_r0, act_r0=act_r0):
                    for ec in range(32):
                        if ec == 0:
                            tensor.wait_ge(s_ld.h, ld_r0 + 16 * 34)
                            for b in range(8):
                                pm.acquire(tensor, pMM[b])
                        tensor.wait_ge(s_lp[ec % 2].h, wch_cnt[ec])
                        if ec == 0:
                            db_order = [x for b in range(8)
                                        for x in (2 * b + 1, 2 * b)]
                        else:
                            db_order = list(range(16))
                        for db in db_order:
                            ins_ = tensor.matmul(
                                pMM[db // 2][:, (db % 2) * RT:
                                             (db % 2 + 1) * RT],
                                wch[ec % 2][:, db * 128:(db + 1) * 128],
                                y_sb[:, ec * RT:(ec + 1) * RT],
                                start=(ec == 0 and db % 2 == 1),
                                stop=(ec == 31))
                        inc(s_pe, ins_)
                    for b in range(8):
                        pm.release(pMM[b], s_act.h, act_r0 + 16)
                pe_omm = s_pe.n

                @block.scalar
                def _(scalar, pe_omm=pe_omm):
                    scalar.wait_ge(s_pe.h, pe_omm)
                    for db in range(16):
                        inc(s_act, scalar.copy(
                            out=fm[:, db * RT:(db + 1) * RT],
                            in_=pMM[db // 2][:, (db % 2) * RT:
                                             (db % 2 + 1) * RT]))
                ev_omm = s_act.n

                for tt in range(2):
                    pe_t0 = s_pe.n

                    @block.tensor
                    def _(tensor, tt=tt, ev_omm=ev_omm):
                        tensor.wait_ge(s_act.h, ev_omm)
                        for g in range(4):
                            bank = pMM[g]
                            pm.acquire(tensor, bank)
                            for m in range(4):
                                k = 4 * g + m
                                ins_ = tensor.transpose(
                                    bank[:, m * 128:(m + 1) * 128],
                                    fm[:, k * RT + tt * 128:
                                       k * RT + (tt + 1) * 128],
                                    ident_f[:])
                            inc(s_pe, ins_)
                            pm.release(bank, s_act.h, s_act.n + g + 1)

                    @block.scalar
                    def _(scalar, pe_t0=pe_t0):
                        for g in range(4):
                            scalar.wait_ge(s_pe.h, pe_t0 + g + 1)
                            inc(s_act, scalar.copy(
                                out=ao_tok.rearrange(
                                    "p (k f) -> p k f", k=16)[
                                    :, 4 * g:4 * g + 4, :],
                                in_=pMM[g].rearrange(
                                    "p (m f) -> p m f", m=4)))
                    ev_tp = s_act.n

                    act_ln0, dve_ln0 = s_act.n, s_dve.n

                    @block.scalar
                    def _(scalar, tt=tt, act_ln0=act_ln0, dve_ln0=dve_ln0):
                        inc(s_act, scalar.activation(
                            out=scr3[:], in_=ao_tok[:], func=AF.Square,
                            accum_out=sm[:, 1:2]))
                        inc(s_act, scalar.activation(
                            out=sm[:, 3:4], in_=sm[:, 1:2], func=AF.Copy,
                            scale=RINV, bias=EPS))
                        inc(s_act, scalar.activation(
                            out=scr3[:], in_=ao_tok[:], func=AF.Copy,
                            accum_out=sm[:, 0:1]))
                        inc(s_act, scalar.activation(
                            out=sm[:, 2:3], in_=sm[:, 0:1], func=AF.Square,
                            scale=RINV))
                        inc(s_act, scalar.activation(
                            out=sm[:, 4:5], in_=sm[:, 2:3], func=AF.Ln,
                            scale=-1.0, bias=sm[:, 3:4]))
                        inc(s_act, scalar.activation(
                            out=sm[:, 5:6], in_=sm[:, 4:5], func=AF.Exp,
                            scale=-0.5))
                        inc(s_act, scalar.activation(
                            out=sm[:, 6:7], in_=sm[:, 0:1], func=AF.Copy,
                            scale=RINV))
                        scalar.wait_ge(s_dve.h, dve_ln0 + 1)
                        inc(s_act, scalar.activation(
                            out=scr3[:], in_=x2t[tt][:], func=AF.Square,
                            accum_out=sm[:, 8:9]))
                        inc(s_act, scalar.activation(
                            out=sm[:, 11:12], in_=sm[:, 8:9], func=AF.Copy,
                            scale=RINV, bias=EPS))
                        inc(s_act, scalar.activation(
                            out=scr3[:], in_=x2t[tt][:], func=AF.Copy,
                            accum_out=sm[:, 7:8]))
                        inc(s_act, scalar.activation(
                            out=sm[:, 10:11], in_=sm[:, 7:8], func=AF.Square,
                            scale=RINV))
                        inc(s_act, scalar.activation(
                            out=sm[:, 12:13], in_=sm[:, 10:11], func=AF.Ln,
                            scale=-1.0, bias=sm[:, 11:12]))
                        inc(s_act, scalar.activation(
                            out=sm[:, 13:14], in_=sm[:, 12:13], func=AF.Exp,
                            scale=-0.5))
                        inc(s_act, scalar.activation(
                            out=sm[:, 14:15], in_=sm[:, 7:8], func=AF.Copy,
                            scale=RINV))

                    @block.vector
                    def _(vector, tt=tt, act_ln0=act_ln0, ev_tp=ev_tp,
                          ld_r0=ld_r0):
                        vector.wait_ge(s_act.h, ev_tp)
                        vector.wait_ge(s_ld.h, ld_r0 + 16 * 34)
                        vector.wait_ge(s_act.h, act_ln0 + 7)
                        vector.tensor_scalar(
                            out=lnt[:], in0=ao_tok[:], scalar1=sm[:, 6:7],
                            scalar2=sm[:, 5:6], op0=ALU.subtract,
                            op1=ALU.mult)
                        inc(s_dve, vector.tensor_add(
                            out=x2t[tt][:], in0=xs3[tt][:], in1=lnt[:]))
                        vector.wait_ge(s_act.h, act_ln0 + 14)
                        inc(s_dve, vector.tensor_scalar(
                            out=xn2t[tt][:], in0=x2t[tt][:],
                            scalar1=sm[:, 14:15], scalar2=sm[:, 13:14],
                            op0=ALU.subtract, op1=ALU.mult))
                    xn2_done = s_dve.n

                    pe_x0 = s_pe.n

                    @block.tensor
                    def _(tensor, tt=tt, xn2_done=xn2_done):
                        tensor.wait_ge(s_dve.h, xn2_done)
                        for g in range(4):
                            bank = pMM[4 + g]
                            pm.acquire(tensor, bank)
                            bb = bank.bitcast(BF16)
                            for m in range(4):
                                k = 4 * g + m
                                ins_ = tensor.transpose(
                                    bb[:, m * 128:(m + 1) * 128],
                                    xn2t[tt][:, k * 128:(k + 1) * 128],
                                    ident_bf[:])
                            inc(s_pe, ins_)
                            pm.release(bank, s_act.h, s_act.n + g + 1)

                    @block.scalar
                    def _(scalar, tt=tt, pe_x0=pe_x0):
                        for g in range(4):
                            scalar.wait_ge(s_pe.h, pe_x0 + g + 1)
                            dst = xn2_fm.rearrange(
                                "p (k f) -> p k f", k=16)[
                                :, 4 * g:4 * g + 4,
                                tt * 128:(tt + 1) * 128]
                            src = pMM[4 + g].bitcast(BF16)[:, 0:512]\
                                .rearrange("p (m f) -> p m f", m=4)
                            inc(s_act, scalar.copy(out=dst, in_=src))
                    ev_x = s_act.n

                    pe_g0 = s_pe.n

                    @block.tensor
                    def _(tensor, tt=tt, ev_x=ev_x):
                        tensor.wait_ge(s_act.h, ev_x)
                        bank = pMM[0]
                        pm.acquire(tensor, bank)
                        for k in range(16):
                            ins_ = tensor.matmul(
                                bank[0:8, 0:128],
                                gate_sb[:, k * 8:(k + 1) * 8],
                                xn2_fm[:, k * RT + tt * 128:
                                       k * RT + (tt + 1) * 128],
                                start=(k == 0), stop=(k == 15))
                        inc(s_pe, ins_)
                        pm.release(bank, s_act.h, s_act.n + 1)

                    @block.scalar
                    def _(scalar, tt=tt, pe_g0=pe_g0):
                        scalar.wait_ge(s_pe.h, pe_g0 + 1)
                        inc(s_act, scalar.copy(
                            out=glog[:, tt * 128:(tt + 1) * 128],
                            in_=pMM[0][0:8, 0:128]))
                    ev_g = s_act.n

                    pe_g1 = s_pe.n

                    @block.tensor
                    def _(tensor, tt=tt, ev_g=ev_g):
                        tensor.wait_ge(s_act.h, ev_g)
                        bank = pMM[1]
                        pm.acquire(tensor, bank)
                        ins_ = tensor.transpose(
                            bank[:, 0:8],
                            glog[:, tt * 128:(tt + 1) * 128],
                            ident_f[0:8, 0:8])
                        inc(s_pe, ins_)
                        pm.release(bank, s_act.h, s_act.n + 1)

                    act_s0, dve_s0 = s_act.n, s_dve.n

                    @block.scalar
                    def _(scalar, pe_g1=pe_g1, dve_s0=dve_s0):
                        scalar.wait_ge(s_pe.h, pe_g1 + 1)
                        inc(s_act, scalar.copy(out=ltok[:, 0:8],
                                               in_=pMM[1][:, 0:8]))
                        scalar.wait_ge(s_dve.h, dve_s0 + 15)
                        inc(s_act, scalar.activation(
                            out=sm[:, 19:20], in_=sm[:, 18:19], func=AF.Exp))
                        scalar.activation(out=sm[:, 20:21], in_=sm[:, 19:20],
                                          func=AF.Ln, bias=cst[:, 1:2])
                        inc(s_act, scalar.activation(
                            out=sm[:, 21:22], in_=sm[:, 20:21], func=AF.Exp,
                            scale=-1.0))

                    @block.vector
                    def _(vector, tt=tt, act_s0=act_s0):
                        def j():
                            inc(s_dve, vector.memset(jnk3[:, 0:1], 0.0))

                        vector.wait_ge(s_act.h, act_s0 + 1)
                        inc(s_dve, vector.reduce_max(
                            out=sm[:, 16:17], in_=ltok[:, 0:8], axis=AX.X))
                        j(); j()
                        inc(s_dve, vector.tensor_scalar(
                            out=ltok[:, 8:16], in0=ltok[:, 0:8],
                            scalar1=sm[:, 16:17], scalar2=None,
                            op0=ALU.is_equal))
                        j(); j()
                        inc(s_dve, vector.scalar_tensor_tensor(
                            out=ltok[:, 8:16], in0=ltok[:, 8:16],
                            scalar=-1e30, in1=ltok[:, 0:8],
                            op0=ALU.mult, op1=ALU.add))
                        j(); j()
                        inc(s_dve, vector.reduce_max(
                            out=sm[:, 17:18], in_=ltok[:, 8:16], axis=AX.X))
                        j(); j()
                        inc(s_dve, vector.tensor_sub(
                            out=sm[:, 18:19], in0=sm[:, 17:18],
                            in1=sm[:, 16:17]))
                        j(); j()
                        vector.wait_ge(s_act.h, act_s0 + 3)
                        inc(s_dve, vector.tensor_mul(
                            out=sm[:, 22:23], in0=sm[:, 19:20],
                            in1=sm[:, 21:22]))
                        j(); j()
                        inc(s_dve, vector.tensor_add(
                            out=sgate[tt][:], in0=sm[:, 21:22],
                            in1=sm[:, 22:23]))
                        j(); j()
                        inc(s_dve, vector.tensor_scalar_mul(
                            out=xn2t[tt][:], in0=xn2t[tt][:],
                            scalar1=sgate[tt][:, 0:1]))
                    s_done = s_dve.n

                    pe_xs = s_pe.n

                    @block.tensor
                    def _(tensor, tt=tt, s_done=s_done):
                        tensor.wait_ge(s_dve.h, s_done)
                        for g in range(4):
                            bank = pMM[4 + g]
                            pm.acquire(tensor, bank)
                            bb = bank.bitcast(BF16)
                            for m in range(4):
                                k = 4 * g + m
                                ins_ = tensor.transpose(
                                    bb[:, m * 128:(m + 1) * 128],
                                    xn2t[tt][:, k * 128:(k + 1) * 128],
                                    ident_bf[:])
                            inc(s_pe, ins_)
                            pm.release(bank, s_act.h, s_act.n + g + 1)

                    @block.scalar
                    def _(scalar, tt=tt, pe_xs=pe_xs):
                        for g in range(4):
                            scalar.wait_ge(s_pe.h, pe_xs + g + 1)
                            dst = xn2s_fm.rearrange(
                                "p (k f) -> p k f", k=16)[
                                :, 4 * g:4 * g + 4,
                                tt * 128:(tt + 1) * 128]
                            src = pMM[4 + g].bitcast(BF16)[:, 0:512]\
                                .rearrange("p (m f) -> p m f", m=4)
                            inc(s_act, scalar.copy(out=dst, in_=src))
                xn_fm_done = s_act.n

                ld_up0, pe_up0 = s_ld.n, s_pe.n

                wup_cnt = []

                @block.sync
                def _(sync, pe_up0=pe_up0):
                    for dc in range(16):
                        if dc < 2:
                            sync.wait_ge(s_pe.h, pe_up0)
                        else:
                            sync.wait_ge(s_pe.h, pe_up0 + dc - 1)
                        wup_cnt.append(inc(
                            s_lp[dc % 2], sync.dma_start(
                                out=wch[dc % 2][:],
                                in_=wup_bf[dc * 128:(dc + 1) * 128, :]), 16))

                act_up0 = s_act.n

                @block.tensor
                def _(tensor, ld_up0=ld_up0, xn_fm_done=xn_fm_done,
                      act_up0=act_up0):
                    tensor.wait_ge(s_act.h, xn_fm_done)
                    for dc in range(16):
                        tensor.wait_ge(s_lp[dc % 2].h, wup_cnt[dc])
                        if dc == 0:
                            for b in range(8):
                                pm.acquire(tensor, pMM[b])
                            ob_order = [x for b in range(8)
                                        for x in (2 * b + 1, 2 * b)]
                        else:
                            ob_order = list(range(16))
                        for ob in ob_order:
                            ins_ = tensor.matmul(
                                pMM[ob // 2][:, (ob % 2) * RT:
                                             (ob % 2 + 1) * RT],
                                wch[dc % 2][:, ob * 128:(ob + 1) * 128],
                                xn2_fm[:, dc * RT:(dc + 1) * RT],
                                start=(dc == 0 and ob % 2 == 1),
                                stop=(dc == 15))
                        inc(s_pe, ins_)
                    for b in range(8):
                        pm.release(pMM[b], s_act.h, act_up0 + 16)
                pe_up_end = s_pe.n

                @block.scalar
                def _(scalar, pe_up_end=pe_up_end):
                    scalar.wait_ge(s_pe.h, pe_up_end)
                    for ob in range(16):
                        inc(s_act, scalar.activation(
                            out=hid[:, ob * RT:(ob + 1) * RT],
                            in_=pMM[ob // 2][:, (ob % 2) * RT:
                                             (ob % 2 + 1) * RT],
                            func=AF.Relu))
                ev_up = s_act.n

                ld_dn0, pe_dn0 = s_ld.n, s_pe.n

                wdn_cnt = []

                @block.sync
                def _(sync, pe_dn0=pe_dn0):
                    for step in range(32):
                        if step < 2:
                            sync.wait_ge(s_pe.h, pe_dn0)
                        else:
                            sync.wait_ge(s_pe.h, pe_dn0 + step - 1)
                        wsrc = wdn_bf if step < 16 else wexp_bf
                        cc_ = step % 16
                        wdn_cnt.append(inc(
                            s_lp[step % 2], sync.dma_start(
                                out=wch[step % 2][:],
                                in_=wsrc[cc_ * 128:(cc_ + 1) * 128, :]), 16))

                act_dn0 = s_act.n

                @block.tensor
                def _(tensor, ld_dn0=ld_dn0, ev_up=ev_up, act_dn0=act_dn0):
                    tensor.wait_ge(s_act.h, ev_up)
                    for step in range(32):
                        tensor.wait_ge(s_lp[step % 2].h, wdn_cnt[step])
                        if step == 0:
                            for b in range(8):
                                pm.acquire(tensor, pMM[b])
                            ob_order = [x for b in range(8)
                                        for x in (2 * b + 1, 2 * b)]
                        else:
                            ob_order = list(range(16))
                        rhs = hid if step < 16 else xn2s_fm
                        cc_ = step % 16
                        for ob in ob_order:
                            ins_ = tensor.matmul(
                                pMM[ob // 2][:, (ob % 2) * RT:
                                             (ob % 2 + 1) * RT],
                                wch[step % 2][:, ob * 128:(ob + 1) * 128],
                                rhs[:, cc_ * RT:(cc_ + 1) * RT],
                                start=(step == 0 and ob % 2 == 1),
                                stop=(step == 31))
                        inc(s_pe, ins_)
                    for b in range(8):
                        pm.release(pMM[b], s_act.h, act_dn0 + 16)
                pe_dn_end = s_pe.n

                @block.scalar
                def _(scalar, pe_dn_end=pe_dn_end):
                    scalar.wait_ge(s_pe.h, pe_dn_end)
                    for db in range(16):
                        inc(s_act, scalar.copy(
                            out=fm[:, db * RT:(db + 1) * RT],
                            in_=pMM[db // 2][:, (db % 2) * RT:
                                             (db % 2 + 1) * RT]))
                ev_ffn = s_act.n

                for tt in range(2):
                    pe_f0 = s_pe.n

                    @block.tensor
                    def _(tensor, tt=tt, ev_ffn=ev_ffn):
                        tensor.wait_ge(s_act.h, ev_ffn)
                        for g in range(4):
                            bank = pMM[g]
                            pm.acquire(tensor, bank)
                            for m in range(4):
                                k = 4 * g + m
                                ins_ = tensor.transpose(
                                    bank[:, m * 128:(m + 1) * 128],
                                    fm[:, k * RT + tt * 128:
                                       k * RT + (tt + 1) * 128],
                                    ident_f[:])
                            inc(s_pe, ins_)
                            pm.release(bank, s_act.h, s_act.n + g + 1)

                    @block.scalar
                    def _(scalar, pe_f0=pe_f0):
                        for g in range(4):
                            scalar.wait_ge(s_pe.h, pe_f0 + g + 1)
                            inc(s_act, scalar.copy(
                                out=ao_tok.rearrange(
                                    "p (k f) -> p k f", k=16)[
                                    :, 4 * g:4 * g + 4, :],
                                in_=pMM[g].rearrange(
                                    "p (m f) -> p m f", m=4)))
                    ev_f = s_act.n

                    act_fl0, dve_fl0 = s_act.n, s_dve.n

                    @block.scalar
                    def _(scalar, act_fl0=act_fl0, dve_fl0=dve_fl0):
                        scalar.wait_ge(s_dve.h, dve_fl0)
                        inc(s_act, scalar.activation(
                            out=scr3[:], in_=ao_tok[:], func=AF.Square,
                            accum_out=sm[:, 1:2]))
                        inc(s_act, scalar.activation(
                            out=sm[:, 3:4], in_=sm[:, 1:2], func=AF.Copy,
                            scale=RINV, bias=EPS))
                        inc(s_act, scalar.activation(
                            out=scr3[:], in_=ao_tok[:], func=AF.Copy,
                            accum_out=sm[:, 0:1]))
                        inc(s_act, scalar.activation(
                            out=sm[:, 2:3], in_=sm[:, 0:1], func=AF.Square,
                            scale=RINV))
                        inc(s_act, scalar.activation(
                            out=sm[:, 4:5], in_=sm[:, 2:3], func=AF.Ln,
                            scale=-1.0, bias=sm[:, 3:4]))
                        inc(s_act, scalar.activation(
                            out=sm[:, 5:6], in_=sm[:, 4:5], func=AF.Exp,
                            scale=-0.5))
                        inc(s_act, scalar.activation(
                            out=sm[:, 6:7], in_=sm[:, 0:1], func=AF.Copy,
                            scale=RINV))

                    st_before = s_st.n

                    @block.vector
                    def _(vector, tt=tt, ev_f=ev_f, act_fl0=act_fl0,
                          st_before=st_before):
                        vector.wait_ge(s_act.h, ev_f)
                        if st_before > 0:
                            vector.wait_ge(s_st.h, st_before)
                        vector.wait_ge(s_act.h, act_fl0 + 7)
                        vector.tensor_scalar(
                            out=lnt[:], in0=ao_tok[:], scalar1=sm[:, 6:7],
                            scalar2=sm[:, 5:6], op0=ALU.subtract,
                            op1=ALU.mult)
                        inc(s_dve, vector.tensor_add(
                            out=out_t[:], in0=x2t[tt][:], in1=lnt[:]))
                    out_done = s_dve.n

                    @block.gpsimd
                    def _(gpsimd, tt=tt, g0=g0, out_done=out_done):
                        gpsimd.wait_ge(s_dve.h, out_done)
                        inc(s_st, gpsimd.dma_start(
                            out=out_tok[g0 + tt * 128:g0 + (tt + 1) * 128, :],
                            in_=out_t[:]), 16)

            @block.sync
            def _(sync):
                sync.wait_ge(s_st.h, s_st.n)

    return nc


_NC_CACHE = {}


def kernel(L0_attn_x, L0_attn_norm_w, L0_attn_W_skip, L0_attn_W_in,
           L0_attn_conv_w, L0_attn_conv_b, L0_attn_Wd1, L0_attn_Wd2,
           L0_attn_bd2, L0_attn_WB, L0_attn_WC, L0_attn_A_log, L0_attn_W_D,
           L0_attn_W_out, L0_ffn_W_up, L0_ffn_W_down, L0_ffn_W_gate,
           L0_ffn_W_expert):
    f = np.float32
    x = np.asarray(L0_attn_x, f).reshape(L, D)
    W_in = np.asarray(L0_attn_W_in, f)
    W_skip = np.asarray(L0_attn_W_skip, f)
    Wd1 = np.asarray(L0_attn_Wd1, f)
    Wd2 = np.asarray(L0_attn_Wd2, f)
    WB = np.asarray(L0_attn_WB, f)
    WC = np.asarray(L0_attn_WC, f)
    A_log = np.asarray(L0_attn_A_log, f)
    conv_w = np.asarray(L0_attn_conv_w, f)
    conv_b = np.asarray(L0_attn_conv_b, f)
    bd2 = np.asarray(L0_attn_bd2, f)
    W_D = np.asarray(L0_attn_W_D, f)
    W_out = np.asarray(L0_attn_W_out, f)
    norm_w = np.asarray(L0_attn_norm_w, f)
    W_up = np.asarray(L0_ffn_W_up, f)
    W_down = np.asarray(L0_ffn_W_down, f)
    W_gate = np.asarray(L0_ffn_W_gate, f)
    W_exp = np.asarray(L0_ffn_W_expert, f)

    ca = np.ascontiguousarray
    w_outT = ca(W_out.T)
    w_upT = ca(W_up.T)
    w_downT = ca(W_down.T)
    w_exp = ca(W_exp)
    w_gateT = ca(W_gate.T)
    ident = np.eye(128, dtype=f)
    nwr = ca(norm_w.reshape(16, 128).T)

    def col4(v):
        return ca(v.reshape(4, 128).T)

    in_maps = []
    for c in range(CORES):
        es = slice(ELOC * c, ELOC * (c + 1))
        in_maps.append(dict(
            x_tok=ca(x[TLOC * c:TLOC * (c + 1), :]),
            w_inT=ca(W_in[es].T),
            w_skipT=ca(W_skip[es].T),
            wd1T=ca(Wd1[:, es].T),
            wd2T=ca(Wd2[es].T),
            wbT=ca(WB[:, es].T),
            wcT=ca(WC[:, es].T),
            alog_r=ca(A_log[es].reshape(4, 128, N).transpose(1, 0, 2)
                      .reshape(128, 4 * N)),
            convw_r=ca(conv_w[es, 0, :].reshape(4, 128, DC)
                       .transpose(1, 0, 2).reshape(128, 4 * DC)),
            convb_r=col4(conv_b[es]),
            bd2_r=col4(bd2[es]),
            wd_r=col4(W_D[es]),
            norm_w=nwr,
            w_outT=w_outT,
            w_upT=w_upT,
            w_downT=w_downT,
            w_exp=w_exp,
            w_gateT=w_gateT,
            ident=ident,
        ))

    if "nc" not in _NC_CACHE:
        _NC_CACHE["nc"] = build_nc()
    nc = _NC_CACHE["nc"]
    res = run_bass_kernel_spmd(nc, in_maps, core_ids=list(range(CORES)))
    out = np.concatenate(
        [np.asarray(res.results[c]["out_tok"], f) for c in range(CORES)], 0)
    return out.reshape(1, L, D)


if __name__ == "__main__":
    import reference as _r
    ins = _r.setup_inputs()
    got = kernel(**{k: np.asarray(v) for k, v in ins.items()})
    want = np.asarray(_r.reference(**ins))
    err = np.abs(got - want)
    rel = float(np.mean((got - want) ** 2) / np.mean(want ** 2))
    print("maxabs", float(err.max()), "relvar", rel)
